# version 11
# baseline (speedup 1.0000x reference)
import numpy as np
import ml_dtypes
import concourse.bass as bass
import concourse.mybir as mybir
from concourse.bass_utils import run_bass_kernel_spmd

F32 = mybir.dt.float32
BF16 = mybir.dt.bfloat16
AF = mybir.ActivationFunctionType
ALU = mybir.AluOpType
AXX = mybir.AxisListType.X
NPBF = ml_dtypes.bfloat16

T, HID, H, D, C = 4096, 1024, 16, 64, 128
NSC = T // C
_cache = {}


def build_nc(nsc=NSC, phase=9.0):
    nc = bass.Bass("TRN2", target_bir_lowering=False)

    def din(name, shape, dtype=F32):
        return nc.dram_tensor(name, list(shape), dtype, kind="ExternalInput").ap()

    x_in = din("x", (T, HID))
    y_out = nc.dram_tensor("y", [T, HID], F32, kind="ExternalOutput").ap()
    W_dram = {n: din(n, (HID, HID)) for n in ["wq", "wk", "wv", "wg", "wo", "fw"]}
    flo = din("flo", (HID, 136)); fv2 = din("fv2", (128, HID))
    wba = din("wba", (HID, 48))
    cw = din("cw", (128, 96)); cb = din("cb", (128, 24))
    smalls = din("smalls", (16, 5)); onw = din("onw", (64, 1))
    maskneg = din("maskneg", (128, 128)); maskpos = din("maskpos", (128, 128))
    dn4 = din("dn4", (16, 512), BF16)
    bo4dpf = din("bo4f", (128, 64))
    bo4b = din("bo4b", (128, 64), BF16)
    ones3 = din("ones3", (3, 128), BF16)
    ones1f = din("ones1f", (1, 128))
    eye16f = din("eye16f", (16, 16))
    idbf = din("idbf", (128, 128), BF16)
    idf32 = din("idf32", (128, 128))
    cs = din("cs", (128, 2 * T), BF16)

    S = {}
    hold = []

    def sb(name, shape, dtype):
        h = nc.sbuf_tensor(name, list(shape), dtype)
        hold.append(h)
        S[name] = h.__enter__()

    for n in ["wq", "wk", "wv", "wg", "wo", "fw"]:
        sb(n + "s", (128, 8192), BF16)
    sb("flos", (128, 8 * 136), BF16)
    sb("fv2s", (128, 1024), BF16)
    sb("wbas", (128, 8 * 48), BF16)
    sb("cws", (128, 96), F32); sb("cbs", (128, 24), F32)
    sb("smalls_s", (16, 5), F32); sb("onw2", (128, 1), F32)
    sb("mneg", (128, 128), F32); sb("mpos", (128, 128), F32)
    sb("dn4_s", (16, 512), BF16); sb("bo4dp", (128, 64), BF16)
    sb("bo4f_s", (128, 64), F32); sb("bo4b_s", (128, 64), BF16)
    sb("ones3_s", (3, 128), BF16)
    sb("idbf_s", (128, 128), BF16); sb("idf32_s", (128, 128), F32)
    sb("stage", (128, 512), F32)
    sb("constv", (128, 8), F32)
    sb("constvb", (128, 4), BF16)
    sb("bias48", (48, 1), F32); sb("negea", (16, 1), F32)
    sb("csb", (128, 256), BF16)
    sb("xin", (128, 1024), F32)
    sb("xt", (128, 1024), F32)
    sb("x2t", (128, 1024), F32)
    sb("lnt", (128, 1024), BF16)
    sb("lnT", (128, 1024), BF16)
    sb("cvb", (128, 8 * 131), F32)
    sb("ab", (128, 1024), F32)
    sb("tails", (128, 72), F32)
    sb("qsil", (128, 1024), BF16); sb("ksil", (128, 1024), BF16)
    sb("vsil", (128, 1024), BF16); sb("gsil", (128, 1024), BF16)
    sb("ktm", (128, 1024), BF16); sb("vtm", (128, 1024), BF16)
    sb("tb", (128, 128), BF16)
    sb("gates", (128, 8), F32)
    sb("r0", (128, 128), BF16)
    sb("rw", (128, 8), F32)
    sb("rsq", (16, 128), F32); sb("rsqb", (16, 128), BF16)
    sb("spall", (48, 128), F32); sb("sp2", (48, 128), F32)
    sb("bfm", (16, 128), F32); sb("lnbfm", (16, 128), F32)
    sb("glogfm", (16, 128), F32)
    sb("Gfm", (16, 129), F32); sb("gplfm", (16, 128), F32)
    sb("t025", (16, 128), F32)
    sb("splg", (16, 3 * 129), BF16); sb("splp", (16, 3 * 129), BF16)
    sb("repA", (3, 2048), BF16); sb("repM", (3, 2048), BF16)
    sb("gtm", (128, 16), F32)
    sb("egprev", (128, 16), F32)
    sb("expdel", (128, 16), BF16)
    sb("btm", (128, 16), BF16); sb("bgetm", (128, 16), BF16)
    sb("gcrow", (1, 16), F32); sb("gam", (64, 16), F32)
    sb("dpb", (16, 1), BF16); sb("dprowb", (1, 16), BF16)
    sb("gc64", (1, 16), F32); sb("gc025", (1, 16), F32)
    sb("gA", (1, 16), BF16); sb("gB", (1, 16), BF16); sb("gC", (1, 16), BF16)
    sb("grow3", (3, 16), BF16)
    sb("arg1", (128, 2048), BF16); sb("E1", (128, 2048), BF16)
    sb("KKm", (128, 2048), BF16)
    sb("MT", (128, 2048), BF16)
    sb("NT0", (128, 2048), BF16); sb("NT1", (128, 2048), BF16)
    sb("NU0", (128, 2048), BF16)
    sb("X0", (128, 2048), BF16); sb("X1", (128, 2048), BF16)
    sb("Kp", (128, 1024), BF16)
    sb("Sf", (64, 1024), F32)
    sb("Sb", (64, 1024), BF16)
    sb("SbE", (128, 1024), BF16); sb("SbO", (128, 1024), BF16)
    sb("qkT", (128, 16), F32)

    def psb(name, shape, dtype):
        h = nc.psum_tensor(name, list(shape), dtype)
        hold.append(h)
        S[name] = h.__enter__()

    psb("psA", (128, 2048), F32)
    psb("psB", (128, 1024), F32)
    psb("psT", (128, 512), BF16)
    psb("psS", (128, 496), F32)
    psA, psB, psT, psS = S["psA"], S["psB"], S["psT"], S["psS"]

    sems = {}
    for k in ["dma", "dw", "dx", "dy", "dcs", "dg", "dk", "gp", "te", "ve", "ac"]:
        h = nc.semaphore(k)
        hold.append(h)
        sems[k] = h.__enter__()
    cnt = {k: 0 for k in sems}
    qmap = {"dw": "dma", "dx": "dma", "dy": "dma", "dcs": "gp", "dg": "gp", "dk": "gp"}
    sched = {k: [] for k in ["dma", "gp", "te", "ve", "ac"]}

    def op(ek, fn, waits=None, inc=1):
        w = {k: v for k, v in (waits or {}).items() if v and v > 0}
        pre = cnt[ek]
        cnt[ek] += inc
        sched[qmap.get(ek, ek)].append((w, fn, inc, ek, pre))
        return dict(cnt)

    _cvals = [0.0, 1.0, -1.0, 1e-5, 1e-6, 1.0 / 64.0, 1e-12, -0.5]
    for i, v in enumerate(_cvals):
        nc.const_aps.register(F32, v, S["constv"][:, i:i + 1])
    for i, v in enumerate([0.0, 1.0, -1.0]):
        nc.const_aps.register(BF16, v, S["constvb"][:, i:i + 1])

    def f_constv(e):
        for i, v in enumerate(_cvals):
            e.memset(S["constv"][:, i:i + 1], v)
        for i, v in enumerate([0.0, 1.0, -1.0]):
            e.memset(S["constvb"][:, i:i + 1], v)
        return e.memset(S["rw"][:], 0.0)
    op("ve", f_constv)

    d1 = op("dma", lambda e: e.dma_start(out=S["onw2"][0:64, :], in_=onw), inc=16)
    d2 = op("dma", lambda e: e.dma_start(out=S["onw2"][64:128, :], in_=onw), inc=16)
    const_specs = [("cws", cw), ("cbs", cb), ("smalls_s", smalls), ("mneg", maskneg),
                   ("mpos", maskpos), ("dn4_s", dn4), ("bo4f_s", bo4dpf), ("bo4b_s", bo4b), ("ones3_s", ones3),
                   ("idbf_s", idbf),
                   ("idf32_s", idf32)]
    for nm, src in const_specs:
        op("dma", (lambda e, nm=nm, src=src: e.dma_start(out=S[nm][:], in_=src)), inc=16)
    consts_done = dict(cnt)

    prev_cast = None
    for wi, n in enumerate(["fw", "wq", "wk", "wv", "wg", "wo"]):
        for j in range(8):
            for hh in range(2):
                w = {"ve": prev_cast["ve"]} if prev_cast else {}
                if n == "wo" and j == 0 and hh == 0:
                    w["dma"] = consts_done["dma"]
                ld = op("dw", (lambda e, n=n, j=j, hh=hh: e.dma_start(
                    out=S["stage"][:],
                    in_=W_dram[n][128 * j:128 * j + 128, 512 * hh:512 * hh + 512])),
                    waits=w, inc=16)
                co = 1024 * j + 512 * hh
                fn = (lambda e, n=n, co=co: e.tensor_copy(
                    out=S[n + "s"][:, co:co + 512], in_=S["stage"][:]))
                prev_cast = op("ve", fn, waits={"dw": ld["dw"]})
    for j in range(8):
        ld = op("dw", (lambda e, j=j: e.dma_start(
            out=S["stage"][:, 0:136], in_=flo[128 * j:128 * j + 128, :])),
            waits={"ve": prev_cast["ve"]}, inc=16)
        prev_cast = op("ve", (lambda e, j=j: e.tensor_copy(
            out=S["flos"][:, 136 * j:136 * j + 136], in_=S["stage"][:, 0:136])),
            waits={"dw": ld["dw"]})
    for j in range(8):
        ld = op("dw", (lambda e, j=j: e.dma_start(
            out=S["stage"][:, 0:48], in_=wba[128 * j:128 * j + 128, :])),
            waits={"ve": prev_cast["ve"]}, inc=16)

        op("ve", (lambda e, j=j: e.tensor_copy(
            out=S["wbas"][:, 48 * j:48 * j + 32], in_=S["stage"][:, 0:32])),
            waits={"dw": ld["dw"]})
        prev_cast = op("ve", (lambda e, j=j: e.tensor_scalar(
            out=S["wbas"][:, 48 * j + 32:48 * j + 48],
            in0=S["stage"][:, 32:48], scalar1=-1.0, scalar2=None, op0=ALU.mult)))
    for hh in range(2):
        ld = op("dw", (lambda e, hh=hh: e.dma_start(
            out=S["stage"][:], in_=fv2[:, 512 * hh:512 * hh + 512])),
            waits={"ve": prev_cast["ve"]}, inc=16)
        prev_cast = op("ve", (lambda e, hh=hh: e.tensor_copy(
            out=S["fv2s"][:, 512 * hh:512 * hh + 512], in_=S["stage"][:])),
            waits={"dw": ld["dw"]})

    op("ve", lambda e: e.memset(S["bias48"][:], 0.0),
       waits={"dma": consts_done["dma"]})
    op("ve", lambda e: e.tensor_tensor(out=S["bias48"][0:16, :],
                                       in0=S["smalls_s"][:, 0:1],
                                       in1=S["smalls_s"][:, 1:2], op=ALU.add))
    bias_done = op("ve", lambda e: e.tensor_scalar(
        out=S["bias48"][32:48, :], in0=S["smalls_s"][:, 2:3],
        scalar1=-1.0, scalar2=None, op0=ALU.mult))
    ea = op("ac", lambda e: e.activation(out=S["negea"][:], in_=S["smalls_s"][:, 3:4],
                                         func=AF.Exp),
            waits={"dma": consts_done["dma"], "ve": 1})
    ea2 = op("ve", lambda e: e.tensor_scalar(out=S["negea"][:], in0=S["negea"][:],
                                             scalar1=-1.0, scalar2=None, op0=ALU.mult),
             waits={"ac": ea["ac"]})
    dpc = op("ve", lambda e: e.tensor_copy(out=S["dpb"][:], in_=S["smalls_s"][:, 4:5]),
             waits={"dma": consts_done["dma"]})
    m1 = op("te", lambda e: e.matmul(out=psS[0:1, 0:16], lhsT=S["dpb"][:],
                                     rhs=S["idbf_s"][0:16, 0:16], start=True, stop=True,
                                     skip_group_check=True),
            waits={"ve": dpc["ve"]})
    e1 = op("ve", lambda e: e.tensor_copy(out=S["dprowb"][:], in_=psS[0:1, 0:16]),
            waits={"te": m1["te"]})
    m2 = op("te", lambda e: e.matmul(out=psS[:, 16:32], lhsT=S["ones3_s"][0:1, :],
                                     rhs=S["dprowb"][:], start=True, stop=True,
                                     skip_group_check=True),
            waits={"ve": e1["ve"]})
    bo4dp_d = op("ve", lambda e: e.tensor_tensor(
        out=S["bo4dp"][:].rearrange("p (i h) -> p i h", i=4),
        in0=S["bo4f_s"][:].rearrange("p (i h) -> p i h", i=4),
        in1=psS[:, 16:32].unsqueeze(1).broadcast_to([128, 4, 16]),
        op=ALU.mult), waits={"te": m2["te"]})

    def f_zero(e):
        for nm in ["SbE", "SbO", "Sb", "Gfm", "splg", "splp", "tails", "cvb"]:
            e.memset(S[nm][:], 0.0)
        return e.memset(S["Sf"][:], 0.0)
    zero_d = op("ve", f_zero, waits={"te": m2["te"]})

    x0 = op("dx", lambda e: e.dma_start(out=S["xin"][:, 0:1024], in_=x_in[0:128, :]),
            inc=16)
    op("dcs", lambda e: e.dma_start(out=S["csb"][:, 0:128], in_=cs[:, 0:128]), inc=16)
    cs0 = op("dcs", lambda e: e.dma_start(out=S["csb"][:, 128:256], in_=cs[:, T:T + 128]),
             inc=16)

    prolog = dict(cnt)
    last_xdma = x0
    last_cs = cs0

    def TS(ek, out, in0, s1, s2=None, op0=ALU.mult, op1=None, waits=None):
        if op1 is None:
            return op(ek, (lambda e: e.tensor_scalar(out=out, in0=in0, scalar1=s1,
                                                     scalar2=s2, op0=op0)), waits=waits)
        return op(ek, (lambda e: e.tensor_scalar(out=out, in0=in0, scalar1=s1, scalar2=s2,
                                                 op0=op0, op1=op1)), waits=waits)

    def TT(ek, out, in0, in1, alu=ALU.mult, waits=None):
        return op(ek, (lambda e: e.tensor_tensor(out=out, in0=in0, in1=in1, op=alu)),
                  waits=waits)

    def MM(out, lhsT, rhs, start, stop, waits=None, tpos=None):
        return op("te", (lambda e: e.matmul(out=out, lhsT=lhsT, rhs=rhs, start=start,
                                            stop=stop, skip_group_check=True,
                                            tile_position=tpos)), waits=waits)

    def TR(out, in_, ident, waits=None):
        return op("te", (lambda e: e.transpose(out=out, in_=in_, identity=ident)),
                  waits=waits)

    def ACT(out, in_, func, bias=0.0, scale=1.0, accum=None, waits=None):
        return op("ac", (lambda e: e.activation(out=out, in_=in_, func=func, bias=bias,
                                                scale=scale, accum_out=accum)),
                  waits=waits)

    def CP(ek, out, in_, waits=None):
        return op(ek, (lambda e: e.tensor_copy(out=out, in_=in_)), waits=waits)

    wq_s, wk_s, wv_s, wg_s = S["wqs"], S["wks"], S["wvs"], S["wgs"]

    for c in range(nsc):
        xin_ap = S["xin"][:]

        def ln(src, wfirst):
            rw = S["rw"]
            a = op("ve", (lambda e: e.reduce_sum(out=rw[:, 0:1], in_=src, axis=AXX)),
                   waits=wfirst)
            TS("ve", rw[:, 0:1], rw[:, 0:1], 1.0 / HID)
            b = ACT(S["ab"][:], src, AF.Square, accum=rw[:, 2:3], waits=wfirst)
            TT("ve", rw[:, 1:2], rw[:, 0:1], rw[:, 0:1], ALU.mult, waits={"ac": b["ac"]})
            op("ve", lambda e: e.scalar_tensor_tensor(
                out=rw[:, 3:4], in0=rw[:, 2:3], scalar=1.0 / HID, in1=rw[:, 1:2],
                op0=ALU.mult, op1=ALU.subtract))
            s1 = ACT(rw[:, 4:5], rw[:, 3:4], AF.Ln, bias=1e-5, waits={"ve": cnt["ve"]})
            s1 = ACT(rw[:, 3:4], rw[:, 4:5], AF.Exp, scale=-0.5)
            lnm = TS("ve", S["lnt"][:], src, rw[:, 0:1], rw[:, 3:4],
                     ALU.subtract, ALU.mult, waits={"ac": s1["ac"]})
            evs = None
            for g in range(2):
                tp = None
                for j in range(4):
                    jj = 4 * g + j
                    w = {"ve": lnm["ve"]}
                    if evs:
                        w["ac"] = evs["ac"]
                    tp = TR(psT[:, 128 * j:128 * j + 128],
                            S["lnt"][:, 128 * jj:128 * jj + 128], S["idbf_s"][:], waits=w)
                evs = op("ac", (lambda e, g=g: e.copy(
                    out=S["lnT"][:, 512 * g:512 * g + 512], in_=psT[:])),
                    waits={"te": tp["te"]})
            return evs

        def ffn(x_src, out_ap, wln):
            mmb = None
            for j in range(8):
                for hh in range(2):
                    mmb = MM(psB[:, 512 * hh:512 * hh + 512],
                             S["lnT"][:, 128 * j:128 * j + 128],
                             S["fws"][:, 1024 * j + 512 * hh:1024 * j + 512 * hh + 512],
                             j == 0, False,
                             waits={"ac": wln["ac"]} if j == 0 and hh == 0 else None)
            for j in range(8):
                mmb = MM(psA[:, 0:136], S["lnT"][:, 128 * j:128 * j + 128],
                         S["flos"][:, 136 * j:136 * j + 136], j == 0, j == 7)
            rw = S["rw"]
            a = op("ve", (lambda e: e.reduce_max(out=rw[:, 5:6], in_=psA[:, 128:136],
                                                 axis=AXX)), waits={"te": mmb["te"]})
            TS("ve", rw[:, 5:6], rw[:, 5:6], -1.0)
            b = op("ac", (lambda e: e.activation(out=S["gates"][:], in_=psA[:, 128:136],
                                                 func=AF.Exp, bias=rw[:, 5:6])),
                   waits={"ve": cnt["ve"]})
            op("ve", (lambda e: e.reduce_sum(out=rw[:, 5:6], in_=S["gates"][:], axis=AXX)),
               waits={"ac": b["ac"]})
            op("ve", lambda e: e.reciprocal(out=rw[:, 5:6], in_=rw[:, 5:6]))
            TS("ve", rw[:, 5:6], rw[:, 5:6], 1.0 / 16.0)
            TS("ve", S["gates"][:], S["gates"][:], rw[:, 5:6])
            tbm = op("ve", lambda e: e.tensor_tensor(
                out=S["tb"][:].rearrange("p (g r) -> p g r", g=8),
                in0=psA[:, 0:128].rearrange("p (g r) -> p g r", g=8),
                in1=S["gates"][:].unsqueeze(-1).broadcast_to([128, 8, 16]),
                op=ALU.mult))
            tp = TR(psT[:, 0:128], S["tb"][:], S["idbf_s"][:],
                    waits={"ve": tbm["ve"], "ac": wln["ac"]})
            ev = op("ac", lambda e: e.copy(out=S["r0"][:], in_=psT[:, 0:128]),
                    waits={"te": tp["te"]})
            lm = MM(psB[:, 0:512], S["r0"][:], S["fv2s"][:, 0:512], False, True,
                    waits={"ac": ev["ac"]})
            lm = MM(psB[:, 512:1024], S["r0"][:], S["fv2s"][:, 512:1024], False, True)
            return op("ve", (lambda e: e.scalar_tensor_tensor(
                out=out_ap, in0=psB[:], scalar=0.5, in1=x_src,
                op0=ALU.mult, op1=ALU.add)), waits={"te": lm["te"]})

        w0 = {"dx": last_xdma["dx"]}
        if c > 0:
            w0["dy"] = last_od["dy"]
        if c == 0:
            w0["ve"] = max(prev_cast["ve"], bias_done["ve"], ea2["ve"], bo4dp_d["ve"],
                           zero_d["ve"])
        else:
            w0["ve"] = sc_end_ve
            w0["te"] = sc_end_te
        ln1 = ln(xin_ap, w0)
        x1d = ffn(xin_ap, S["xt"][:], ln1)

        if phase <= 1:
            sc_end_ve = cnt["ve"]; sc_end_te = cnt["te"]
            continue
        if c + 1 < nsc:
            last_xdma = op("dx", (lambda e, c=c: e.dma_start(
                out=S["xin"][:],
                in_=x_in[128 * (c + 1):128 * (c + 1) + 128, :])),
                waits={"ve": x1d["ve"]}, inc=16)

        ln2 = ln(S["xt"][:], {"ve": x1d["ve"]})

        prev_ev = {"ve": cnt["ve"]}
        sil_d = {}
        ab_guard = {"ac": ln2["ac"]}
        for wn, pn, pi in [("wq", "q", 0), ("wk", "k", 1), ("wv", "v", 2), ("wg", "g", 3)]:
            mm = None
            for ot in range(8):
                for j in range(8):
                    w = None
                    if ot == 0 and j == 0:
                        w = {"ac": ln2["ac"], "ve": prev_ev["ve"]}
                    mm = MM(psB[:, 128 * ot:128 * ot + 128],
                            S[wn + "s"][:, 1024 * j + 128 * ot:1024 * j + 128 * ot + 128],
                            S["lnT"][:, 128 * j:128 * j + 128], j == 0, j == 7, waits=w)
            if pn == "g":
                prev_ev = CP("ve", S["gsil"][:], psB[:], waits={"te": mm["te"]})
                continue
            cvr = S["cvb"][:].rearrange("p (b w) -> p b w", w=131)
            t_new = op("ve", (lambda e, pi=pi: e.tensor_copy(
                out=S["ab"][:, 0:24].rearrange("p (b w) -> p b w", w=3),
                in_=psB[:].rearrange("p (b w) -> p b w", w=128)[:, :, 125:128])),
                waits={"te": mm["te"], "ac": ab_guard["ac"]})
            ev1 = op("ve", (lambda e: e.tensor_copy(
                out=S["cvb"][:].rearrange("p (b w) -> p b w", w=131)[:, :, 3:131],
                in_=psB[:].rearrange("p (b w) -> p b w", w=128))))
            ev2 = op("ve", (lambda e, pi=pi: e.tensor_copy(
                out=S["cvb"][:].rearrange("p (b w) -> p b w", w=131)[:, :, 0:3],
                in_=S["tails"][:, 24 * pi:24 * pi + 24]
                .rearrange("p (b w) -> p b w", w=3))))
            ev3 = op("ve", (lambda e, pi=pi: e.tensor_copy(
                out=S["tails"][:, 24 * pi:24 * pi + 24], in_=S["ab"][:, 0:24])))
            dst = {"q": "qsil", "k": "ksil", "v": "vsil"}[pn]
            st = None
            for bkt in range(8):
                base = 131 * bkt
                cwc = S["cws"][:, 12 * bkt + 4 * pi:12 * bkt + 4 * pi + 4]
                st = TS("ve", S["ab"][:, 128 * bkt:128 * bkt + 128],
                        S["cvb"][:, base + 0:base + 128], cwc[:, 0:1],
                        waits={"ve": ev3["ve"]} if bkt == 0 else None)
                for tap in range(1, 4):
                    st = op("ve", (lambda e, bkt=bkt, tap=tap, base=base, cwc=cwc:
                                   e.scalar_tensor_tensor(
                        out=S["ab"][:, 128 * bkt:128 * bkt + 128],
                        in0=S["cvb"][:, base + tap:base + tap + 128],
                        scalar=cwc[:, tap:tap + 1],
                        in1=S["ab"][:, 128 * bkt:128 * bkt + 128],
                        op0=ALU.mult, op1=ALU.add)))
            sl = None
            for bkt in range(8):
                sl = op("ac", (lambda e, bkt=bkt, pi=pi, dst=dst: e.activation(
                    out=S[dst][:, 128 * bkt:128 * bkt + 128],
                    in_=S["ab"][:, 128 * bkt:128 * bkt + 128], func=AF.Silu,
                    bias=S["cbs"][:, 3 * bkt + pi:3 * bkt + pi + 1])),
                    waits={"ve": st["ve"]} if bkt == 0 else None)
            sil_d[pn] = sl
            ab_guard = sl
            prev_ev = ev1

        if phase <= 3:
            sc_end_ve = cnt["ve"]; sc_end_te = cnt["te"]
            continue
        mm = None
        for j in range(8):
            mm = MM(psS[0:48, 0:128], S["wbas"][:, 48 * j:48 * j + 48],
                    S["lnT"][:, 128 * j:128 * j + 128], j == 0, j == 7,
                    waits={"ac": ln2["ac"]} if j == 0 else None)
        zw = {"te": mm["te"]}
        ACT(S["spall"][:], psS[0:48, 0:128], AF.Abs, bias=S["bias48"][:], waits=zw)
        ACT(S["KKm"][:].bitcast(F32)[0:48, 0:128], S["spall"][:], AF.Exp, scale=-1.0)
        a3 = ACT(S["spall"][:], S["KKm"][:].bitcast(F32)[0:48, 0:128], AF.Ln, bias=1.0)
        a4 = ACT(S["KKm"][:].bitcast(F32)[0:48, 0:128], psS[0:48, 0:128], AF.Relu, bias=S["bias48"][:],
                 waits=zw)
        a5 = TT("ve", S["sp2"][:], S["spall"][:], S["KKm"][:].bitcast(F32)[0:48, 0:128], ALU.add,
                waits={"ac": a4["ac"]})
        TS("ve", S["lnbfm"][:], S["sp2"][32:48, :], -1.0, waits={"ve": a5["ve"]})
        bsg = ACT(S["bfm"][:], S["sp2"][32:48, :], AF.Exp, scale=-1.0,
                  waits={"ve": a5["ve"]})
        glg = TS("ve", S["glogfm"][:], S["sp2"][0:16, :], S["negea"][:, 0:1])
        gsc = op("ve", lambda e: e.tensor_tensor_scan(
            out=S["Gfm"][:, 1:129], data0=S["glogfm"][:], data1=S["glogfm"][:],
            initial=0.0, op0=ALU.add, op1=ALU.bypass), waits={"ve": glg["ve"]})
        gpl = TT("ve", S["gplfm"][:], S["Gfm"][:, 1:129], S["lnbfm"][:], ALU.add)

        def split3(src, dst):
            TS("ve", S["rsq"][:], src, 2.0 ** 29, 2.0 ** 29, ALU.add, ALU.subtract)
            TS("ve", S["t025"][:], src, 2.0 ** 21, 2.0 ** 21, ALU.add, ALU.subtract)
            CP("ve", dst[:, 1:129], S["rsq"][:])
            TT("ve", dst[:, 130:258], S["t025"][:], S["rsq"][:], ALU.subtract)
            return TT("ve", dst[:, 259:387], src, S["t025"][:], ALU.subtract)
        spg = split3(S["Gfm"][:, 1:129], S["splg"])
        spp = split3(S["gplfm"][:], S["splp"])
        fd = None
        for j in range(3):
            fd = op("gp", (lambda e, j=j: e.dma_start(
                out=S["repA"][j:j + 1, :],
                in_=S["splp"][:, 129 * j + 1:129 * j + 129])),
                waits={"ve": spp["ve"]}, inc=16)
            fd = op("gp", (lambda e, j=j: e.dma_start(
                out=S["repM"][j:j + 1, :],
                in_=S["splg"][:, 129 * j + 0:129 * j + 128])),
                waits={"ve": spg["ve"]}, inc=16)
        rep_dma = fd

        tws = {"ve": gpl["ve"], "ac": bsg["ac"]}
        TR(psS[:, 128:144], S["Gfm"][:, 1:129], S["idf32_s"][0:16, 0:16], waits=tws)
        TR(psS[:, 144:160], S["Gfm"][:, 0:128], S["idf32_s"][0:16, 0:16])
        TR(psS[:, 160:176], S["gplfm"][:], S["idf32_s"][0:16, 0:16])
        t4 = TR(psS[:, 176:192], S["bfm"][:], S["idf32_s"][0:16, 0:16])
        gtm_ev = CP("ve", S["gtm"][:], psS[:, 128:144], waits={"te": t4["te"]})
        egp = ACT(S["egprev"][:], psS[:, 144:160], AF.Exp, waits={"te": t4["te"]})
        bge = ACT(S["bgetm"][:], psS[:, 160:176], AF.Exp, waits={"te": t4["te"]})
        btm_ev = CP("ve", S["btm"][:], psS[:, 176:192], waits={"te": t4["te"]})
        g1 = op("dg", lambda e: e.dma_start(out=S["gcrow"][:], in_=S["Gfm"][:, 128:129]),
                waits={"ve": gsc["ve"]}, inc=16)
        g2a = TS("ve", S["gc64"][:], S["gcrow"][:], 2.0 ** 29, 2.0 ** 29, ALU.add,
                 ALU.subtract, waits={"dg": g1["dg"]})
        TS("ve", S["gc025"][:], S["gcrow"][:], 2.0 ** 21, 2.0 ** 21, ALU.add, ALU.subtract)
        CP("ve", S["gA"][:], S["gc64"][:])
        TT("ve", S["gB"][:], S["gc025"][:], S["gc64"][:], ALU.subtract)
        g2b = TT("ve", S["gC"][:], S["gcrow"][:], S["gc025"][:], ALU.subtract)
        op("dg", lambda e: e.dma_start(out=S["grow3"][0:1, :], in_=S["gA"][:]),
           waits={"ve": g2b["ve"]}, inc=16)
        op("dg", lambda e: e.dma_start(out=S["grow3"][1:2, :], in_=S["gB"][:]), inc=16)
        g2c = op("dg", lambda e: e.dma_start(out=S["grow3"][2:3, :], in_=S["gC"][:]),
                 inc=16)
        g3 = MM(psS[:, 208:224], S["ones3_s"][:], S["grow3"][:], True, True,
                waits={"dg": g2c["dg"]})
        ad = TT("ve", S["KKm"][:].bitcast(F32)[:, 16:32], psS[:, 208:224], S["gtm"][:], ALU.subtract,
                waits={"te": g3["te"], "ve": gtm_ev["ve"]})
        exd = ACT(S["expdel"][:], S["KKm"][:].bitcast(F32)[:, 16:32], AF.Exp, waits={"ve": ad["ve"]})
        gm = ACT(S["gam"][:], psS[0:64, 208:224], AF.Exp, waits={"te": g3["te"]})

        if phase <= 2:
            sc_end_ve = cnt["ve"]; sc_end_te = cnt["te"]
            continue
        csl = S["csb"][:, 0:128]
        ssl = S["csb"][:, 128:256]
        csw = {"dcs": last_cs["dcs"]}
        prev_mm_guard = None
        for srcn, dstn in [("qsil", "qsil"), ("ksil", "ksil")]:
            w1 = {"ac": sil_d["k" if srcn == "ksil" else "q"]["ac"]}
            w1.update(csw)
            if prev_mm_guard is not None:
                w1["te"] = prev_mm_guard
            v6 = None
            for pr in range(4):
                Eap = S[srcn][:, 128 * pr:128 * pr + 128]
                Oap = S[srcn][:, 128 * (4 + pr):128 * (4 + pr) + 128]
                TT("ve", S["X0"][:, 1024:1152], Eap, csl, ALU.mult,
                   waits=w1 if pr == 0 else None)
                TT("ve", S["X0"][:, 1152:1280], Eap, ssl, ALU.mult)
                TT("ve", S["X0"][:, 1280:1408], Oap, ssl, ALU.mult)
                TT("ve", S["X0"][:, 1408:1536], Oap, csl, ALU.mult)
                TT("ve", S["X1"][:, 128 * pr:128 * pr + 128], S["X0"][:, 1024:1152],
                   S["X0"][:, 1280:1408], ALU.subtract)
                v6 = TT("ve", S["X1"][:, 128 * (4 + pr):128 * (4 + pr) + 128],
                        S["X0"][:, 1152:1280], S["X0"][:, 1408:1536], ALU.add)
            sq = None
            for j in range(8):
                sq = ACT(S["X0"][:, 128 * j:128 * j + 128],
                         S["X1"][:, 128 * j:128 * j + 128], AF.Square,
                         waits={"ve": v6["ve"]} if j == 0 else None)
            sqmm = None
            for j in range(8):
                sqmm = MM(psS[0:16, 224:352], S["bo4b_s"][:, 16 * (j % 4):16 * (j % 4) + 16],
                          S["X0"][:, 128 * j:128 * j + 128], j == 0, j == 7,
                          waits={"ac": sq["ac"]} if j == 0 else None)
            rq2 = ACT(S["sp2"][0:16, :], psS[0:16, 224:352], AF.Ln, bias=1e-12,
                      waits={"te": sqmm["te"]})
            rq3 = ACT(S["rsq"][:], S["sp2"][0:16, :], AF.Exp, scale=-0.5)
            rq4 = CP("ve", S["rsqb"][:], S["rsq"][:], waits={"ac": rq3["ac"]})
            rp = None
            for pr in range(4):
                rp = MM(psA[:, 128 * pr:128 * pr + 128],
                        S["dn4_s"][:, 128 * pr:128 * pr + 128],
                        S["rsqb"][:], True, True,
                        waits={"ve": rq4["ve"]} if pr == 0 else None)
            rc = op("ac", lambda e: e.copy(out=S["E1"][:, 0:512], in_=psA[:, 0:512]),
                    waits={"te": rp["te"]})
            lsq = None
            for pr in range(4):
                TT("ve", S[dstn][:, 128 * pr:128 * pr + 128],
                   S["X1"][:, 128 * pr:128 * pr + 128],
                   S["E1"][:, 128 * pr:128 * pr + 128], ALU.mult,
                   waits={"ac": rc["ac"]} if pr == 0 else None)
                lsq = TT("ve", S[dstn][:, 128 * (4 + pr):128 * (4 + pr) + 128],
                         S["X1"][:, 128 * (4 + pr):128 * (4 + pr) + 128],
                         S["E1"][:, 128 * pr:128 * pr + 128], ALU.mult)
            prev_mm_guard = rp["te"]
        ql2 = lsq

        if c + 1 < nsc:
            last_cs = op("dcs", (lambda e, c=c: e.dma_start(
                out=S["csb"][:, 0:128],
                in_=cs[:, 128 * (c + 1):128 * (c + 1) + 128])),
                waits={"ve": ql2["ve"]}, inc=16)
            last_cs = op("dcs", (lambda e, c=c: e.dma_start(
                out=S["csb"][:, 128:256],
                in_=cs[:, T + 128 * (c + 1):T + 128 * (c + 1) + 128])), inc=16)

        if phase <= 4:
            sc_end_ve = cnt["ve"]; sc_end_te = cnt["te"]
            continue
        evk = None
        for g in range(4):
            src = S["ksil"] if g < 2 else S["vsil"]
            dstn = "ktm" if g < 2 else "vtm"
            gg = g % 2
            tp = None
            for j in range(4):
                jj = 4 * gg + j
                w = {"ve": ql2["ve"]} if g == 0 and j == 0 else None
                if evk and j == 0:
                    w = {"ac": evk["ac"]}
                tp = TR(psT[:, 128 * j:128 * j + 128], src[:, 128 * jj:128 * jj + 128],
                        S["idbf_s"][:], waits=w)
            evk = op("ac", (lambda e, dstn=dstn, gg=gg: e.copy(
                out=S[dstn][:, 512 * gg:512 * gg + 512], in_=psT[:])),
                waits={"te": tp["te"]})

        if phase <= 4.2:
            sc_end_ve = cnt["ve"]; sc_end_te = cnt["te"]
            continue
        kd = None
        for (srcn, dstn, rr) in [("ksil", "X0", 0), ("qsil", "X1", 0)]:
            for a in range(4):
                for eo in range(2):
                    w = {"ve": ql2["ve"]} if kd is None else None
                    kd = op("dk", (lambda e, srcn=srcn, dstn=dstn, a=a, eo=eo: e.dma_start(
                        out=S[dstn][32 * eo:32 * eo + 32, :]
                        .rearrange("p (m c) -> p m c", m=16)[:, a::4, :],
                        in_=S[srcn][32 * a:32 * a + 32, 512 * eo:512 * eo + 512]
                        .rearrange("p (b c) -> p b c", b=4))), waits=w, inc=16)
        krs_dma = kd

        mmr = None
        for seg in range(4):
            mmr = MM(psA[:, 512 * seg:512 * seg + 512], S["ones3_s"][:],
                     S["repA"][:, 512 * seg:512 * seg + 512], True, True,
                     waits={"gp": rep_dma["gp"], "ac": evk["ac"]} if seg == 0 else None)
        ats = None
        for h in range(H):
            ats = TS("ve", S["arg1"][:, 128 * h:128 * h + 128],
                     psA[:, 128 * h:128 * h + 128], S["gtm"][:, h:h + 1], 0.0,
                     ALU.subtract, ALU.min,
                     waits={"te": mmr["te"], "ve": gtm_ev["ve"]} if h == 0 else None)
        eA = ACT(S["E1"][:], S["arg1"][:], AF.Exp, waits={"ve": ats["ve"]})
        if phase <= 4.4:
            sc_end_ve = cnt["ve"]; sc_end_te = cnt["te"]
            continue
        mmk = None
        for h in range(H):
            kS = S["X0"][0:64, 128 * h:128 * h + 128]
            w = {"ve": ats["ve"], "dk": krs_dma["dk"]} if h == 0 else None
            mmk = MM(psA[:, 128 * h:128 * h + 128], kS, kS, True, True, waits=w)
        if phase <= 4.5:
            sc_end_ve = cnt["ve"]; sc_end_te = cnt["te"]
            continue
        kkev = op("ve", lambda e: e.tensor_tensor(
            out=S["KKm"][:].rearrange("p (h t) -> p h t", h=16),
            in0=psA[:].rearrange("p (h t) -> p h t", h=16),
            in1=S["mneg"][:].unsqueeze(1).broadcast_to([128, 16, 128]),
            op=ALU.mult), waits={"te": mmk["te"]})
        nt0 = TT("ve", S["NT0"][:], S["E1"][:], S["KKm"][:], ALU.mult,
                 waits={"ac": eA["ac"]})
        if phase <= 4.6:
            sc_end_ve = cnt["ve"]; sc_end_te = cnt["te"]
            continue
        mmr = None
        for seg in range(4):
            mmr = MM(psA[:, 512 * seg:512 * seg + 512], S["ones3_s"][:],
                     S["repM"][:, 512 * seg:512 * seg + 512], True, True,
                     waits={"ve": kkev["ve"]} if seg == 0 else None)
        ats = None
        for h in range(H):
            ats = TS("ve", S["arg1"][:, 128 * h:128 * h + 128],
                     psA[:, 128 * h:128 * h + 128], S["gtm"][:, h:h + 1], 0.0,
                     ALU.subtract, ALU.min, waits={"te": mmr["te"]} if h == 0 else None)
        eM = ACT(S["E1"][:], S["arg1"][:], AF.Exp, waits={"ve": ats["ve"]})
        mmk = None
        for h in range(H):
            kS = S["X0"][0:64, 128 * h:128 * h + 128]
            qS = S["X1"][0:64, 128 * h:128 * h + 128]
            w = {"ve": ats["ve"]} if h == 0 else None
            mmk = MM(psA[:, 128 * h:128 * h + 128], kS, qS, True, True, waits=w)
        if phase <= 4.8:
            sc_end_ve = cnt["ve"]; sc_end_te = cnt["te"]
            continue
        kqev = op("ve", lambda e: e.tensor_tensor(
            out=S["KKm"][:].rearrange("p (h t) -> p h t", h=16),
            in0=psA[:].rearrange("p (h t) -> p h t", h=16),
            in1=S["mpos"][:].unsqueeze(1).broadcast_to([128, 16, 128]),
            op=ALU.mult), waits={"te": mmk["te"]})
        mtd = TT("ve", S["MT"][:], S["E1"][:], S["KKm"][:], ALU.mult,
                 waits={"ac": eM["ac"]})

        if phase <= 5:
            sc_end_ve = cnt["ve"]; sc_end_te = cnt["te"]
            continue
        xb1 = op("ve", lambda e: e.tensor_tensor(
            out=S["X0"][:].rearrange("p (h z) -> p h z", h=16)[:, :, 0:64],
            in0=S["vtm"][:].rearrange("p (h d) -> p h d", h=16),
            in1=S["btm"][:].unsqueeze(-1).broadcast_to([128, 16, 64]),
            op=ALU.mult), waits={"ve": mtd["ve"], "ac": evk["ac"]})
        xb2 = op("ve", lambda e: e.tensor_tensor(
            out=S["X0"][:].rearrange("p (b a z) -> p b a z", b=4, a=4)[:, :, :, 64:128]
            .rearrange("p b a (eo d) -> p b a eo d", eo=2),
            in0=S["ktm"][:].rearrange("p (eo b a d) -> p b a eo d", eo=2, b=4, a=4),
            in1=S["bgetm"][:].rearrange("p (b a) -> p b a", b=4)
            .unsqueeze(-1).unsqueeze(-1).broadcast_to([128, 4, 4, 2, 32]),
            op=ALU.mult))
        kp1 = op("ve", lambda e: e.tensor_tensor(
            out=S["Kp"][:].rearrange("p (b a z) -> p b a z", b=4, a=4)
            .rearrange("p b a (eo d) -> p b a eo d", eo=2),
            in0=S["ktm"][:].rearrange("p (eo b a d) -> p b a eo d", eo=2, b=4, a=4),
            in1=S["expdel"][:].rearrange("p (b a) -> p b a", b=4)
            .unsqueeze(-1).unsqueeze(-1).broadcast_to([128, 4, 4, 2, 32]),
            op=ALU.mult), waits={"ac": exd["ac"]})

        evn = None
        for g in range(4):
            tp = None
            for j in range(4):
                jj = 4 * g + j
                w = {"ve": nt0["ve"]} if g == 0 and j == 0 else None
                if evn and j == 0:
                    w = {"ac": evn["ac"]}
                tp = TR(psT[:, 128 * j:128 * j + 128],
                        S["NT0"][:, 128 * jj:128 * jj + 128], S["idbf_s"][:], waits=w)
            evn = op("ac", (lambda e, g=g: e.copy(
                out=S["NU0"][:, 512 * g:512 * g + 512], in_=psT[:])),
                waits={"te": tp["te"]})

        NTc, NTn = "NT0", "NT1"
        Xc, Xn = "X0", "X1"
        last_ac = evn
        last_ve = {"ve": max(xb1["ve"], xb2["ve"], kp1["ve"])}
        def halves(src_a, src_b_name, dst_fn, evac_eng, wait0, alu_or_copy):
            mms = [None, None]
            evs = [None, None]
            for half in range(2):
                for i in range(8):
                    h = 8 * half + i
                    w = None
                    if half == 0 and i == 0:
                        w = dict(wait0)
                    if half == 1 and i == 0 and evs[0] is None:
                        w = None
                    mms[half] = MM(psA[:, 128 * h:128 * h + 128],
                                   src_a(h), S[src_b_name][:, 128 * h:128 * h + 128],
                                   True, True, waits=w)
                evs[half] = dst_fn(half, mms[half])
            return evs[1]
        for lvl in range(7):
            w0l = {"ac": last_ac["ac"], "ve": last_ve["ve"]}
            mmA = [None, None]
            for half in range(2):
                for i in range(8):
                    h = 8 * half + i
                    w = dict(w0l) if (half == 0 and i == 0) else None
                    if half == 1 and i == 0:
                        w = None
                    mmA[half] = MM(psA[:, 128 * h:128 * h + 128],
                                   S[NTc][:, 128 * h:128 * h + 128],
                                   S[Xc][:, 128 * h:128 * h + 128], True, True, waits=w)
            x1e = TT("ve", S[Xn][:, 0:1024], psA[:, 0:1024], S[Xc][:, 0:1024], ALU.add,
                     waits={"te": mmA[0]["te"]})
            xev = TT("ve", S[Xn][:, 1024:2048], psA[:, 1024:2048], S[Xc][:, 1024:2048],
                     ALU.add, waits={"te": mmA[1]["te"]})
            last_ve = xev
            if lvl < 6:
                mmB = [None, None]
                for half in range(2):
                    for i in range(8):
                        h = 8 * half + i
                        w = None
                        if half == 0 and i == 0:
                            w = {"ve": x1e["ve"]}
                        if half == 1 and i == 0:
                            w = {"ve": xev["ve"]}
                        mmB[half] = MM(psA[:, 128 * h:128 * h + 128],
                                       S["NU0"][:, 128 * h:128 * h + 128],
                                       S[NTc][:, 128 * h:128 * h + 128], True, True,
                                       waits=w)
                n1 = op("ac", (lambda e, NTn=NTn: e.copy(out=S[NTn][:, 0:1024],
                                                         in_=psA[:, 0:1024])),
                        waits={"te": mmB[0]["te"]})
                ntc = op("ac", (lambda e, NTn=NTn: e.copy(out=S[NTn][:, 1024:2048],
                                                          in_=psA[:, 1024:2048])),
                        waits={"te": mmB[1]["te"]})
                last_ac = ntc
                if lvl < 5:
                    mmC = [None, None]
                    for half in range(2):
                        for i in range(8):
                            h = 8 * half + i
                            w = None
                            if half == 0 and i == 0:
                                w = {"ac": n1["ac"]}
                            if half == 1 and i == 0:
                                w = {"ac": ntc["ac"]}
                            mmC[half] = MM(psA[:, 128 * h:128 * h + 128],
                                           S[NTc][:, 128 * h:128 * h + 128],
                                           S["NU0"][:, 128 * h:128 * h + 128], True,
                                           True, waits=w)
                    u1 = op("ac", lambda e: e.copy(out=S["NU0"][:, 0:1024],
                                                   in_=psA[:, 0:1024]),
                            waits={"te": mmC[0]["te"]})
                    evn2 = op("ac", lambda e: e.copy(out=S["NU0"][:, 1024:2048],
                                                     in_=psA[:, 1024:2048]),
                             waits={"te": mmC[1]["te"]})
                    last_ac = evn2
            NTc, NTn = NTn, NTc
            Xc, Xn = Xn, Xc
        Xf = Xc

        if phase <= 6:
            sc_end_ve = cnt["ve"]; sc_end_te = cnt["te"]
            continue
        mmw = None
        for h in range(H):
            w = {"ve": last_ve["ve"]} if h == 0 else None
            mmw = MM(psA[0:64, 128 * h:128 * h + 128],
                     S[Xf][:, 128 * h + 64:128 * h + 128],
                     S["MT"][:, 128 * h:128 * h + 128], True, True, waits=w)
        nmk = TS("ve", S["arg1"][0:64, :], psA[0:64, :], -1.0, waits={"te": mmw["te"]})
        mmw = None
        for h in range(H):
            w = {"ve": nmk["ve"]} if h == 0 else None
            mmw = MM(psB[0:64, 64 * h:64 * h + 64],
                     S[Xf][:, 128 * h + 64:128 * h + 128],
                     S["Kp"][:, 64 * h:64 * h + 64], True, True, waits=w)
        nuk = TS("ve", S["E1"][0:64, 0:1024], psB[0:64, :], -1.0, waits={"te": mmw["te"]})

        mq = None
        for h in range(H):
            a, b2 = h % 4, h // 4
            w = {"ve": nuk["ve"]} if h == 0 else None
            mq = MM(psA[:, 128 * h:128 * h + 64], S["qsil"][:, 128 * b2:128 * b2 + 128],
                    S["SbE"][:, 64 * h:64 * h + 64], True, False, waits=w)
            mq = MM(psA[:, 128 * h:128 * h + 64],
                    S["qsil"][:, 512 + 128 * b2:512 + 128 * b2 + 128],
                    S["SbO"][:, 64 * h:64 * h + 64], False, True)
        mf = None
        for h in range(H):
            mf = MM(psB[:, 64 * h:64 * h + 64], S["arg1"][0:64, :][:, 128 * h:128 * h + 128],
                    S["Sb"][:, 64 * h:64 * h + 64], True, False)
            mf = MM(psB[:, 64 * h:64 * h + 64], S["MT"][:, 128 * h:128 * h + 128],
                    S[Xf][:, 128 * h:128 * h + 64], False, True)
        fmev = op("ac", lambda e: e.copy(out=S["ab"][:], in_=psB[:]),
                  waits={"te": mf["te"]})
        oev = None
        for h in range(H):
            w = {"ac": fmev["ac"], "te": mq["te"]} if h == 0 else None
            oev = op("ve", (lambda e, h=h: e.scalar_tensor_tensor(
                out=S["cvb"][:, 64 * h:64 * h + 64], in0=psA[:, 128 * h:128 * h + 64],
                scalar=S["egprev"][:, h:h + 1], in1=S["ab"][:, 64 * h:64 * h + 64],
                op0=ALU.mult, op1=ALU.add)), waits=w)


        mb = None
        for h in range(H):
            w = {"ve": oev["ve"]} if h == 0 else None
            mb = MM(psB[0:64, 64 * h:64 * h + 64], S["E1"][0:64, 0:1024][:, 64 * h:64 * h + 64],
                    S["Sb"][:, 64 * h:64 * h + 64], True, False, waits=w)
            mb = MM(psB[0:64, 64 * h:64 * h + 64], S["Kp"][:, 64 * h:64 * h + 64],
                    S[Xf][:, 128 * h:128 * h + 64], False, True)
        sup = None
        for h in range(H):
            w = {"te": mb["te"], "ac": gm["ac"]} if h == 0 else None
            sup = op("ve", (lambda e, h=h: e.scalar_tensor_tensor(
                out=S["Sf"][:, 64 * h:64 * h + 64], in0=S["Sf"][:, 64 * h:64 * h + 64],
                scalar=S["gam"][:, h:h + 1], in1=psB[0:64, 64 * h:64 * h + 64],
                op0=ALU.mult, op1=ALU.add)), waits=w)
        sbc = CP("ve", S["Sb"][:], S["Sf"][:], waits={"ve": sup["ve"]})
        pad = None
        for r in range(4):
            pad = op("ve", (lambda e, r=r: e.tensor_copy(
                out=S["SbE"][32 * r:32 * r + 32, :]
                .rearrange("p (m q) -> p m q", m=4)[:, :, 64 * r:64 * r + 64],
                in_=S["Sb"][0:32, :]
                .rearrange("p (m q) -> p m q", m=4)[:, :, 64 * r:64 * r + 64])))
            pad = op("ve", (lambda e, r=r: e.tensor_copy(
                out=S["SbO"][32 * r:32 * r + 32, :]
                .rearrange("p (m q) -> p m q", m=4)[:, :, 64 * r:64 * r + 64],
                in_=S["Sb"][32:64, :]
                .rearrange("p (m q) -> p m q", m=4)[:, :, 64 * r:64 * r + 64])))

        if phase <= 7:
            sc_end_ve = cnt["ve"]; sc_end_te = cnt["te"]
            continue
        prod = None
        for pr in range(8):
            prod = TT("ve", S["KKm"][:, 128 * pr:128 * pr + 128],
                      S["qsil"][:, 128 * pr:128 * pr + 128],
                      S["ksil"][:, 128 * pr:128 * pr + 128], ALU.mult)
        mmd = None
        for pr in range(8):
            mmd = MM(psS[0:16, 224:352], S["bo4dp"][:, 16 * (pr % 4):16 * (pr % 4) + 16],
                     S["KKm"][:, 128 * pr:128 * pr + 128], pr == 0, pr == 7,
                     waits={"ve": prod["ve"]} if pr == 0 else None)
        qk1 = CP("ve", S["rsq"][:], psS[0:16, 224:352], waits={"te": mmd["te"]})
        qtr = TR(psS[:, 352:368], S["rsq"][:], S["idf32_s"][0:16, 0:16], waits={"ve": qk1["ve"]})
        qk2 = CP("ve", S["qkT"][:], psS[:, 352:368], waits={"te": qtr["te"]})
        dpv = None
        for h in range(H):
            w = {"ve": qk2["ve"]} if h == 0 else None
            dp1 = op("ve", (lambda e, h=h: e.tensor_scalar(
                out=S["ab"][:, 64 * h:64 * h + 64], in0=S["vtm"][:, 64 * h:64 * h + 64],
                scalar1=S["qkT"][:, h:h + 1], scalar2=None, op0=ALU.mult)), waits=w)
            dpv = TT("ve", S["cvb"][:, 64 * h:64 * h + 64],
                     S["cvb"][:, 64 * h:64 * h + 64], S["ab"][:, 64 * h:64 * h + 64],
                     ALU.add)

        sqo = ACT(S["ab"][:], S["cvb"][:, 0:1024][:], AF.Square, waits={"ve": dpv["ve"]})
        rs1 = op("ve", lambda e: e.tensor_reduce(
            out=S["egprev"][:], in_=S["ab"][:].rearrange("p (h d) -> p h d", h=16),
            axis=AXX, op=ALU.add), waits={"ac": sqo["ac"]})
        rs2 = ACT(S["qkT"][:], S["egprev"][:], AF.Ln, bias=1e-6, scale=1.0 / 64.0,
                  waits={"ve": rs1["ve"]})
        rs3x = ACT(S["egprev"][:], S["qkT"][:], AF.Exp, scale=-0.5)
        orv = None
        for h in range(H):
            w = {"ac": rs3x["ac"]} if h == 0 else None
            orv = op("ve", (lambda e, h=h: e.tensor_scalar(
                out=S["qsil"][:, 64 * h:64 * h + 64], in0=S["cvb"][:, 64 * h:64 * h + 64],
                scalar1=S["egprev"][:, h:h + 1], scalar2=None, op0=ALU.mult)), waits=w)
        evo = None
        for g in range(2):
            tp = None
            for j in range(4):
                jj = 4 * g + j
                w = {"ve": orv["ve"]} if g == 0 and j == 0 else None
                if evo and j == 0:
                    w = {"ac": evo["ac"]}
                tp = TR(psT[:, 128 * j:128 * j + 128], S["qsil"][:, 128 * jj:128 * jj + 128],
                        S["idbf_s"][:], waits=w)
            evo = op("ac", (lambda e, g=g: e.copy(
                out=S["X0"][:, 512 * g:512 * g + 512], in_=psT[:])),
                waits={"te": tp["te"]})
        slw = ACT(S["X1"][:, 0:1024], S["X0"][:, 0:1024], AF.Silu,
                  scale=S["onw2"][:, 0:1], waits={"ac": evo["ac"]})
        ofd = TT("ve", S["ksil"][:], S["X1"][:, 0:1024], S["gsil"][:], ALU.mult,
                 waits={"ac": slw["ac"]})

        mmo = None
        for j in range(8):
            w = {"ve": ofd["ve"]} if j == 0 else None
            mmo = MM(psB[:, 0:512], S["ksil"][:, 128 * j:128 * j + 128],
                     S["wos"][:, 1024 * j:1024 * j + 512], j == 0, j == 7, waits=w)
            mmo = MM(psB[:, 512:1024], S["ksil"][:, 128 * j:128 * j + 128],
                     S["wos"][:, 1024 * j + 512:1024 * j + 1024], j == 0, j == 7)
        x2d = TT("ve", S["x2t"][:], psB[:], S["xt"][:], ALU.add, waits={"te": mmo["te"]})

        if phase <= 8:
            sc_end_ve = cnt["ve"]; sc_end_te = cnt["te"]
            continue
        ln3 = ln(S["x2t"][:], {"ve": x2d["ve"]})
        yd = ffn(S["x2t"][:], S["xt"][:], ln3)
        od = op("dy", (lambda e, c=c: e.dma_start(
            out=y_out[128 * c:128 * c + 128, :], in_=S["xt"][:])),
            waits={"ve": yd["ve"]}, inc=16)
        last_od = od
        sc_end_ve = yd["ve"]
        sc_end_te = cnt["te"]

    final_dy = cnt["dy"]
    op("dma", (lambda e: e.nop()), waits={"dy": final_dy}, inc=0)

    with nc.Block() as blk:
        engmap = {"te": blk.tensor, "ve": blk.vector, "ac": blk.scalar,
                  "dma": blk.sync, "gp": blk.gpsimd}

        def make(ek):
            def run(e):
                for (w, fn, inc, sk, pre) in sched[ek]:
                    if sk in ("ve", "ac") and pre > 0:
                        e.wait_ge(sems[sk], pre)
                    for k2, v in w.items():
                        e.wait_ge(sems[k2], v)
                    ins = fn(e)
                    if inc:
                        ins.then_inc(sems[sk], inc)
            return run
        for ek in ["te", "ve", "ac", "dma", "gp"]:
            engmap[ek](make(ek))

    for h in reversed(hold):
        h.__exit__(None, None, None)
    return nc


def _host_inputs(inp):
    f32 = np.float32
    A = {k[8:]: np.asarray(v, f32) for k, v in inp.items() if k.startswith("L0_attn_")}
    F = {k[7:]: np.asarray(v, f32) for k, v in inp.items() if k.startswith("L0_ffn_")}
    perm = np.empty(1024, np.int64)
    for r in range(1024):
        eo = r // 512
        rr = r % 512
        a, p = rr // 128, rr % 128
        h, j = 4 * a + p // 32, p % 32
        perm[r] = 64 * h + 2 * j + eo
    d = {}
    d["wq"] = np.ascontiguousarray(A["Wq"][perm].T)
    d["wk"] = np.ascontiguousarray(A["Wk"][perm].T)
    d["wv"] = np.ascontiguousarray(A["Wv"].T)
    d["wg"] = np.ascontiguousarray(A["Wg"].T)
    d["wo"] = np.ascontiguousarray(A["Wo"].T)
    d["fw"] = np.ascontiguousarray(F["weight"].T)
    U2 = F["lora_U"].reshape(128, 1024)
    d["flo"] = np.ascontiguousarray(np.concatenate([U2.T, F["gate_w"].T], axis=1))
    d["fv2"] = np.ascontiguousarray(np.moveaxis(F["lora_V"], 2, 1).reshape(128, 1024))
    d["wba"] = np.ascontiguousarray(np.concatenate(
        [A["Wgk"].T, np.zeros((1024, 16), f32), A["Wb"].T], axis=1))
    cwq = A["cqw"][perm, 0]; cwk = A["ckw"][perm, 0]; cwv = A["cvw"][:, 0]
    cbq = A["cqb"][perm]; cbk = A["ckb"][perm]; cbv = A["cvb"]
    cwh = np.zeros((128, 96), f32)
    cbh = np.zeros((128, 24), f32)
    for bkt in range(8):
        for pi, (wv_, bv_) in enumerate([(cwq, cbq), (cwk, cbk), (cwv, cbv)]):
            cwh[:, 12 * bkt + 4 * pi:12 * bkt + 4 * pi + 4] = wv_[128 * bkt:128 * bkt + 128]
            cbh[:, 3 * bkt + pi] = bv_[128 * bkt:128 * bkt + 128]
    d["cw"] = cwh; d["cb"] = cbh
    d["smalls"] = np.ascontiguousarray(np.stack(
        [A["bgk"], A["dt_bias"], A["bb"], A["A_log"], A["Dp"]], axis=1))
    d["onw"] = np.ascontiguousarray(A["onorm_w"][:, None])
    st = np.triu(np.ones((128, 128), f32), 1)
    d["maskneg"] = -st
    d["maskpos"] = st
    dn = np.zeros((16, 512), NPBF)
    for a in range(4):
        for p in range(128):
            dn[4 * a + p // 32, 128 * a + p] = 1
    d["dn4"] = dn
    bo4t = np.ascontiguousarray(dn.astype(f32).reshape(16, 4, 128)
                                .transpose(2, 1, 0).reshape(128, 64))
    d["bo4f"] = bo4t
    d["bo4b"] = bo4t.astype(NPBF)
    d["ones3"] = np.ones((3, 128), NPBF)
    d["ones1f"] = np.ones((1, 128), f32)
    d["eye16f"] = np.eye(16, dtype=f32)
    d["idbf"] = np.eye(128, dtype=NPBF)
    d["idf32"] = np.eye(128, dtype=f32)
    invf = (1.0 / 10000.0 ** (np.arange(0, D, 2, dtype=f32) / D))
    tt = np.arange(T, dtype=f32)
    j32 = (2 * (np.arange(128) % 32)) % 32
    ang = np.outer(invf[j32], tt)
    d["cs"] = np.ascontiguousarray(
        np.concatenate([np.cos(ang), np.sin(ang)], axis=1).astype(NPBF))
    return d


def kernel(**inputs):
    shared = _host_inputs(inputs)
    x = np.asarray(inputs["L0_attn_x"], np.float32)
    if "nc" not in _cache:
        _cache["nc"] = build_nc()
    nc = _cache["nc"]
    in_maps = []
    for b in range(8):
        m = dict(shared)
        m["x"] = np.ascontiguousarray(x[b])
        in_maps.append(m)
    res = run_bass_kernel_spmd(nc, in_maps, core_ids=list(range(8)))
    out = np.stack([r["y"] for r in res.results], axis=0)
    return out.astype(np.float32)


# version 12
# speedup vs baseline: 1.1276x; 1.1276x over previous
import numpy as np
import ml_dtypes
import concourse.bass as bass
import concourse.mybir as mybir
from concourse.bass_utils import run_bass_kernel_spmd

F32 = mybir.dt.float32
BF16 = mybir.dt.bfloat16
AF = mybir.ActivationFunctionType
ALU = mybir.AluOpType
AXX = mybir.AxisListType.X
NPBF = ml_dtypes.bfloat16

T, HID, H, D, C = 4096, 1024, 16, 64, 128
NSC = T // C
_cache = {}


def build_nc(nsc=NSC, phase=9.0):
    nc = bass.Bass("TRN2", target_bir_lowering=False)

    def din(name, shape, dtype=F32):
        return nc.dram_tensor(name, list(shape), dtype, kind="ExternalInput").ap()

    x_in = din("x", (T, HID))
    y_out = nc.dram_tensor("y", [T, HID], F32, kind="ExternalOutput").ap()
    W_dram = {n: din(n, (HID, HID)) for n in ["wq", "wk", "wv", "wg", "wo", "fw"]}
    flo = din("flo", (HID, 136)); fv2 = din("fv2", (128, HID))
    wba = din("wba", (HID, 48))
    cw = din("cw", (128, 96)); cb = din("cb", (128, 24))
    smalls = din("smalls", (16, 5)); onw = din("onw", (64, 1))
    maskneg = din("maskneg", (128, 128)); maskpos = din("maskpos", (128, 128))
    dn4 = din("dn4", (16, 512), BF16)
    bo4dpf = din("bo4f", (128, 64))
    bo4b = din("bo4b", (128, 64), BF16)
    ones3 = din("ones3", (3, 128), BF16)
    ones1f = din("ones1f", (1, 128))
    eye16f = din("eye16f", (16, 16))
    idbf = din("idbf", (128, 128), BF16)
    idf32 = din("idf32", (128, 128))
    cs = din("cs", (128, 2 * T), BF16)

    S = {}
    hold = []

    def sb(name, shape, dtype):
        h = nc.sbuf_tensor(name, list(shape), dtype)
        hold.append(h)
        S[name] = h.__enter__()

    for n in ["wq", "wk", "wv", "wg", "wo", "fw"]:
        sb(n + "s", (128, 8192), BF16)
    sb("flos", (128, 8 * 136), BF16)
    sb("fv2s", (128, 1024), BF16)
    sb("wbas", (128, 8 * 48), BF16)
    sb("cws", (128, 96), F32); sb("cbs", (128, 24), F32)
    sb("smalls_s", (16, 5), F32); sb("onw2", (128, 1), F32)
    sb("mneg", (128, 128), F32); sb("mpos", (128, 128), F32)
    sb("dn4_s", (16, 512), BF16); sb("bo4dp", (128, 64), BF16)
    sb("bo4f_s", (128, 64), F32); sb("bo4b_s", (128, 64), BF16)
    sb("ones3_s", (3, 128), BF16)
    sb("idbf_s", (128, 128), BF16); sb("idf32_s", (128, 128), F32)
    sb("stage", (128, 512), F32)
    sb("constv", (128, 8), F32)
    sb("constvb", (128, 4), BF16)
    sb("bias48", (48, 1), F32); sb("negea", (16, 1), F32)
    sb("csb", (128, 256), BF16)
    sb("xin", (128, 1024), F32)
    sb("xt", (128, 1024), F32)
    sb("x2t", (128, 1024), F32)
    sb("lnt", (128, 1024), BF16)
    sb("lnT", (128, 1024), BF16)
    sb("cvb", (128, 8 * 131), F32)
    sb("ab", (128, 1024), F32)
    sb("tails", (128, 72), F32)
    sb("qsil", (128, 1024), BF16); sb("ksil", (128, 1024), BF16)
    sb("vsil", (128, 1024), BF16); sb("gsil", (128, 1024), BF16)
    sb("ktm", (128, 1024), BF16); sb("vtm", (128, 1024), BF16)
    sb("tb", (128, 128), BF16)
    sb("gates", (128, 8), F32)
    sb("r0", (128, 128), BF16)
    sb("rw", (128, 8), F32)
    sb("rsq", (16, 128), F32); sb("rsqb", (16, 128), BF16)
    sb("spall", (48, 128), F32); sb("sp2", (48, 128), F32)
    sb("bfm", (16, 128), F32); sb("lnbfm", (16, 128), F32)
    sb("glogfm", (16, 128), F32)
    sb("Gfm", (16, 129), F32); sb("gplfm", (16, 128), F32)
    sb("t025", (16, 128), F32)
    sb("splg", (16, 3 * 129), BF16); sb("splp", (16, 3 * 129), BF16)
    sb("repA", (3, 2048), BF16); sb("repM", (3, 2048), BF16)
    sb("gtm", (128, 16), F32)
    sb("egprev", (128, 16), F32)
    sb("expdel", (128, 16), BF16)
    sb("btm", (128, 16), BF16); sb("bgetm", (128, 16), BF16)
    sb("gcrow", (1, 16), F32); sb("gam", (64, 16), F32)
    sb("dpb", (16, 1), BF16); sb("dprowb", (1, 16), BF16)
    sb("gc64", (1, 16), F32); sb("gc025", (1, 16), F32)
    sb("gA", (1, 16), BF16); sb("gB", (1, 16), BF16); sb("gC", (1, 16), BF16)
    sb("grow3", (3, 16), BF16)
    sb("arg1", (128, 2048), BF16); sb("E1", (128, 2048), BF16)
    sb("KKm", (128, 2048), BF16)
    sb("MT", (128, 2048), BF16)
    sb("NT0", (128, 2048), BF16); sb("NT1", (128, 2048), BF16)
    sb("NU0", (128, 2048), BF16)
    sb("X0", (128, 2048), BF16); sb("X1", (128, 2048), BF16)
    sb("Kp", (128, 1024), BF16)
    sb("Sf", (64, 1024), F32)
    sb("Sb", (64, 1024), BF16)
    sb("SbE", (128, 1024), BF16); sb("SbO", (128, 1024), BF16)
    sb("qkT", (128, 16), F32)

    def psb(name, shape, dtype):
        h = nc.psum_tensor(name, list(shape), dtype)
        hold.append(h)
        S[name] = h.__enter__()

    psb("psA", (128, 2048), F32)
    psb("psB", (128, 1024), F32)
    psb("psT", (128, 512), BF16)
    psb("psS", (128, 496), F32)
    psA, psB, psT, psS = S["psA"], S["psB"], S["psT"], S["psS"]

    sems = {}
    for k in ["dma", "dw", "dx", "dy", "dcs", "dg", "dk", "gp", "te", "ve", "ac"]:
        h = nc.semaphore(k)
        hold.append(h)
        sems[k] = h.__enter__()
    cnt = {k: 0 for k in sems}
    qmap = {"dw": "dma", "dx": "dma", "dy": "dma", "dcs": "gp", "dg": "gp", "dk": "gp"}
    sched = {k: [] for k in ["dma", "gp", "te", "ve", "ac"]}

    def op(ek, fn, waits=None, inc=1):
        w = {k: v for k, v in (waits or {}).items() if v and v > 0}
        pre = cnt[ek]
        cnt[ek] += inc
        sched[qmap.get(ek, ek)].append((w, fn, inc, ek, pre))
        return dict(cnt)

    _cvals = [0.0, 1.0, -1.0, 1e-5, 1e-6, 1.0 / 64.0, 1e-12, -0.5]
    for i, v in enumerate(_cvals):
        nc.const_aps.register(F32, v, S["constv"][:, i:i + 1])
    for i, v in enumerate([0.0, 1.0, -1.0]):
        nc.const_aps.register(BF16, v, S["constvb"][:, i:i + 1])

    def f_constv(e):
        for i, v in enumerate(_cvals):
            e.memset(S["constv"][:, i:i + 1], v)
        for i, v in enumerate([0.0, 1.0, -1.0]):
            e.memset(S["constvb"][:, i:i + 1], v)
        return e.memset(S["rw"][:], 0.0)
    op("ve", f_constv)

    d1 = op("dma", lambda e: e.dma_start(out=S["onw2"][0:64, :], in_=onw), inc=16)
    d2 = op("dma", lambda e: e.dma_start(out=S["onw2"][64:128, :], in_=onw), inc=16)
    const_specs = [("cws", cw), ("cbs", cb), ("smalls_s", smalls), ("mneg", maskneg),
                   ("mpos", maskpos), ("dn4_s", dn4), ("bo4f_s", bo4dpf), ("bo4b_s", bo4b), ("ones3_s", ones3),
                   ("idbf_s", idbf),
                   ("idf32_s", idf32)]
    for nm, src in const_specs:
        op("dma", (lambda e, nm=nm, src=src: e.dma_start(out=S[nm][:], in_=src)), inc=16)
    consts_done = dict(cnt)

    prev_cast = None
    for wi, n in enumerate(["fw", "wq", "wk", "wv", "wg", "wo"]):
        for j in range(8):
            for hh in range(2):
                w = {"ve": prev_cast["ve"]} if prev_cast else {}
                if n == "wo" and j == 0 and hh == 0:
                    w["dma"] = consts_done["dma"]
                ld = op("dw", (lambda e, n=n, j=j, hh=hh: e.dma_start(
                    out=S["stage"][:],
                    in_=W_dram[n][128 * j:128 * j + 128, 512 * hh:512 * hh + 512])),
                    waits=w, inc=16)
                co = 1024 * j + 512 * hh
                fn = (lambda e, n=n, co=co: e.tensor_copy(
                    out=S[n + "s"][:, co:co + 512], in_=S["stage"][:]))
                prev_cast = op("ve", fn, waits={"dw": ld["dw"]})
    for j in range(8):
        ld = op("dw", (lambda e, j=j: e.dma_start(
            out=S["stage"][:, 0:136], in_=flo[128 * j:128 * j + 128, :])),
            waits={"ve": prev_cast["ve"]}, inc=16)
        prev_cast = op("ve", (lambda e, j=j: e.tensor_copy(
            out=S["flos"][:, 136 * j:136 * j + 136], in_=S["stage"][:, 0:136])),
            waits={"dw": ld["dw"]})
    for j in range(8):
        ld = op("dw", (lambda e, j=j: e.dma_start(
            out=S["stage"][:, 0:48], in_=wba[128 * j:128 * j + 128, :])),
            waits={"ve": prev_cast["ve"]}, inc=16)

        op("ve", (lambda e, j=j: e.tensor_copy(
            out=S["wbas"][:, 48 * j:48 * j + 32], in_=S["stage"][:, 0:32])),
            waits={"dw": ld["dw"]})
        prev_cast = op("ve", (lambda e, j=j: e.tensor_scalar(
            out=S["wbas"][:, 48 * j + 32:48 * j + 48],
            in0=S["stage"][:, 32:48], scalar1=-1.0, scalar2=None, op0=ALU.mult)))
    for hh in range(2):
        ld = op("dw", (lambda e, hh=hh: e.dma_start(
            out=S["stage"][:], in_=fv2[:, 512 * hh:512 * hh + 512])),
            waits={"ve": prev_cast["ve"]}, inc=16)
        prev_cast = op("ve", (lambda e, hh=hh: e.tensor_copy(
            out=S["fv2s"][:, 512 * hh:512 * hh + 512], in_=S["stage"][:])),
            waits={"dw": ld["dw"]})

    op("ve", lambda e: e.memset(S["bias48"][:], 0.0),
       waits={"dma": consts_done["dma"]})
    op("ve", lambda e: e.tensor_tensor(out=S["bias48"][0:16, :],
                                       in0=S["smalls_s"][:, 0:1],
                                       in1=S["smalls_s"][:, 1:2], op=ALU.add))
    bias_done = op("ve", lambda e: e.tensor_scalar(
        out=S["bias48"][32:48, :], in0=S["smalls_s"][:, 2:3],
        scalar1=-1.0, scalar2=None, op0=ALU.mult))
    ea = op("ac", lambda e: e.activation(out=S["negea"][:], in_=S["smalls_s"][:, 3:4],
                                         func=AF.Exp),
            waits={"dma": consts_done["dma"], "ve": 1})
    ea2 = op("ve", lambda e: e.tensor_scalar(out=S["negea"][:], in0=S["negea"][:],
                                             scalar1=-1.0, scalar2=None, op0=ALU.mult),
             waits={"ac": ea["ac"]})
    dpc = op("ve", lambda e: e.tensor_copy(out=S["dpb"][:], in_=S["smalls_s"][:, 4:5]),
             waits={"dma": consts_done["dma"]})
    m1 = op("te", lambda e: e.matmul(out=psS[0:1, 0:16], lhsT=S["dpb"][:],
                                     rhs=S["idbf_s"][0:16, 0:16], start=True, stop=True,
                                     skip_group_check=True),
            waits={"ve": dpc["ve"]})
    e1 = op("ve", lambda e: e.tensor_copy(out=S["dprowb"][:], in_=psS[0:1, 0:16]),
            waits={"te": m1["te"]})
    m2 = op("te", lambda e: e.matmul(out=psS[:, 16:32], lhsT=S["ones3_s"][0:1, :],
                                     rhs=S["dprowb"][:], start=True, stop=True,
                                     skip_group_check=True),
            waits={"ve": e1["ve"]})
    bo4dp_d = op("ve", lambda e: e.tensor_tensor(
        out=S["bo4dp"][:].rearrange("p (i h) -> p i h", i=4),
        in0=S["bo4f_s"][:].rearrange("p (i h) -> p i h", i=4),
        in1=psS[:, 16:32].unsqueeze(1).broadcast_to([128, 4, 16]),
        op=ALU.mult), waits={"te": m2["te"]})

    def f_zero(e):
        for nm in ["SbE", "SbO", "Sb", "Gfm", "splg", "splp", "tails", "cvb"]:
            e.memset(S[nm][:], 0.0)
        return e.memset(S["Sf"][:], 0.0)
    zero_d = op("ve", f_zero, waits={"te": m2["te"]})

    x0 = op("dx", lambda e: e.dma_start(out=S["xin"][:, 0:1024], in_=x_in[0:128, :]),
            inc=16)
    op("dcs", lambda e: e.dma_start(out=S["csb"][:, 0:128], in_=cs[:, 0:128]), inc=16)
    cs0 = op("dcs", lambda e: e.dma_start(out=S["csb"][:, 128:256], in_=cs[:, T:T + 128]),
             inc=16)

    prolog = dict(cnt)
    last_xdma = x0
    last_cs = cs0

    def TS(ek, out, in0, s1, s2=None, op0=ALU.mult, op1=None, waits=None):
        if op1 is None:
            return op(ek, (lambda e: e.tensor_scalar(out=out, in0=in0, scalar1=s1,
                                                     scalar2=s2, op0=op0)), waits=waits)
        return op(ek, (lambda e: e.tensor_scalar(out=out, in0=in0, scalar1=s1, scalar2=s2,
                                                 op0=op0, op1=op1)), waits=waits)

    def TT(ek, out, in0, in1, alu=ALU.mult, waits=None):
        return op(ek, (lambda e: e.tensor_tensor(out=out, in0=in0, in1=in1, op=alu)),
                  waits=waits)

    def MM(out, lhsT, rhs, start, stop, waits=None, tpos=None):
        return op("te", (lambda e: e.matmul(out=out, lhsT=lhsT, rhs=rhs, start=start,
                                            stop=stop, skip_group_check=True,
                                            tile_position=tpos)), waits=waits)

    def TR(out, in_, ident, waits=None):
        return op("te", (lambda e: e.transpose(out=out, in_=in_, identity=ident)),
                  waits=waits)

    def ACT(out, in_, func, bias=0.0, scale=1.0, accum=None, waits=None):
        return op("ac", (lambda e: e.activation(out=out, in_=in_, func=func, bias=bias,
                                                scale=scale, accum_out=accum)),
                  waits=waits)

    def CP(ek, out, in_, waits=None):
        return op(ek, (lambda e: e.tensor_copy(out=out, in_=in_)), waits=waits)

    wq_s, wk_s, wv_s, wg_s = S["wqs"], S["wks"], S["wvs"], S["wgs"]

    for c in range(nsc):
        xin_ap = S["xin"][:]

        def ln(src, wfirst):
            rw = S["rw"]
            a = op("ve", (lambda e: e.reduce_sum(out=rw[:, 0:1], in_=src, axis=AXX)),
                   waits=wfirst)
            TS("ve", rw[:, 0:1], rw[:, 0:1], 1.0 / HID)
            b = ACT(S["ab"][:], src, AF.Square, accum=rw[:, 2:3], waits=wfirst)
            TT("ve", rw[:, 1:2], rw[:, 0:1], rw[:, 0:1], ALU.mult, waits={"ac": b["ac"]})
            op("ve", lambda e: e.scalar_tensor_tensor(
                out=rw[:, 3:4], in0=rw[:, 2:3], scalar=1.0 / HID, in1=rw[:, 1:2],
                op0=ALU.mult, op1=ALU.subtract))
            s1 = ACT(rw[:, 4:5], rw[:, 3:4], AF.Ln, bias=1e-5, waits={"ve": cnt["ve"]})
            s1 = ACT(rw[:, 3:4], rw[:, 4:5], AF.Exp, scale=-0.5)
            lnm = TS("ve", S["lnt"][:], src, rw[:, 0:1], rw[:, 3:4],
                     ALU.subtract, ALU.mult, waits={"ac": s1["ac"]})
            evs = None
            for g in range(2):
                tp = None
                for j in range(4):
                    jj = 4 * g + j
                    w = {"ve": lnm["ve"]}
                    if evs:
                        w["ac"] = evs["ac"]
                    tp = TR(psT[:, 128 * j:128 * j + 128],
                            S["lnt"][:, 128 * jj:128 * jj + 128], S["idbf_s"][:], waits=w)
                evs = op("ac", (lambda e, g=g: e.copy(
                    out=S["lnT"][:, 512 * g:512 * g + 512], in_=psT[:])),
                    waits={"te": tp["te"]})
            return evs

        def ffn(x_src, out_ap, wln):
            mmb = None
            for j in range(8):
                for hh in range(2):
                    mmb = MM(psB[:, 512 * hh:512 * hh + 512],
                             S["lnT"][:, 128 * j:128 * j + 128],
                             S["fws"][:, 1024 * j + 512 * hh:1024 * j + 512 * hh + 512],
                             j == 0, False,
                             waits={"ac": wln["ac"]} if j == 0 and hh == 0 else None)
            for j in range(8):
                mmb = MM(psA[:, 0:136], S["lnT"][:, 128 * j:128 * j + 128],
                         S["flos"][:, 136 * j:136 * j + 136], j == 0, j == 7)
            rw = S["rw"]
            a = op("ve", (lambda e: e.reduce_max(out=rw[:, 5:6], in_=psA[:, 128:136],
                                                 axis=AXX)), waits={"te": mmb["te"]})
            TS("ve", rw[:, 5:6], rw[:, 5:6], -1.0)
            b = op("ac", (lambda e: e.activation(out=S["gates"][:], in_=psA[:, 128:136],
                                                 func=AF.Exp, bias=rw[:, 5:6])),
                   waits={"ve": cnt["ve"]})
            op("ve", (lambda e: e.reduce_sum(out=rw[:, 5:6], in_=S["gates"][:], axis=AXX)),
               waits={"ac": b["ac"]})
            op("ve", lambda e: e.reciprocal(out=rw[:, 5:6], in_=rw[:, 5:6]))
            TS("ve", rw[:, 5:6], rw[:, 5:6], 1.0 / 16.0)
            TS("ve", S["gates"][:], S["gates"][:], rw[:, 5:6])
            tbm = op("ve", lambda e: e.tensor_tensor(
                out=S["tb"][:].rearrange("p (g r) -> p g r", g=8),
                in0=psA[:, 0:128].rearrange("p (g r) -> p g r", g=8),
                in1=S["gates"][:].unsqueeze(-1).broadcast_to([128, 8, 16]),
                op=ALU.mult))
            tp = TR(psT[:, 0:128], S["tb"][:], S["idbf_s"][:],
                    waits={"ve": tbm["ve"], "ac": wln["ac"]})
            ev = op("ac", lambda e: e.copy(out=S["r0"][:], in_=psT[:, 0:128]),
                    waits={"te": tp["te"]})
            lm = MM(psB[:, 0:512], S["r0"][:], S["fv2s"][:, 0:512], False, True,
                    waits={"ac": ev["ac"]})
            lm = MM(psB[:, 512:1024], S["r0"][:], S["fv2s"][:, 512:1024], False, True)
            return op("ve", (lambda e: e.scalar_tensor_tensor(
                out=out_ap, in0=psB[:], scalar=0.5, in1=x_src,
                op0=ALU.mult, op1=ALU.add)), waits={"te": lm["te"]})

        w0 = {"dx": last_xdma["dx"]}
        if c > 0:
            w0["dy"] = last_od["dy"]
        if c == 0:
            w0["ve"] = max(prev_cast["ve"], bias_done["ve"], ea2["ve"], bo4dp_d["ve"],
                           zero_d["ve"])
        else:
            w0["ve"] = sc_end_ve
            w0["te"] = sc_end_te
        ln1 = ln(xin_ap, w0)
        x1d = ffn(xin_ap, S["xt"][:], ln1)

        if phase <= 1:
            sc_end_ve = cnt["ve"]; sc_end_te = cnt["te"]
            continue
        if c + 1 < nsc:
            last_xdma = op("dx", (lambda e, c=c: e.dma_start(
                out=S["xin"][:],
                in_=x_in[128 * (c + 1):128 * (c + 1) + 128, :])),
                waits={"ve": x1d["ve"]}, inc=16)

        ln2 = ln(S["xt"][:], {"ve": x1d["ve"]})

        mm = None
        for j in range(8):
            mm = MM(psS[0:48, 0:128], S["wbas"][:, 48 * j:48 * j + 48],
                    S["lnT"][:, 128 * j:128 * j + 128], j == 0, j == 7,
                    waits={"ac": ln2["ac"]} if j == 0 else None)
        zw = {"te": mm["te"]}
        ACT(S["spall"][:], psS[0:48, 0:128], AF.Abs, bias=S["bias48"][:], waits=zw)
        ACT(S["ab"][0:48, 0:128], S["spall"][:], AF.Exp, scale=-1.0)
        a3 = ACT(S["spall"][:], S["ab"][0:48, 0:128], AF.Ln, bias=1.0)
        a4 = ACT(S["ab"][0:48, 0:128], psS[0:48, 0:128], AF.Relu, bias=S["bias48"][:],
                 waits=zw)
        a5 = TT("ve", S["sp2"][:], S["spall"][:], S["ab"][0:48, 0:128], ALU.add,
                waits={"ac": a4["ac"]})
        TS("ve", S["lnbfm"][:], S["sp2"][32:48, :], -1.0, waits={"ve": a5["ve"]})
        bsg = ACT(S["bfm"][:], S["sp2"][32:48, :], AF.Exp, scale=-1.0,
                  waits={"ve": a5["ve"]})
        glg = TS("ve", S["glogfm"][:], S["sp2"][0:16, :], S["negea"][:, 0:1])
        gsc = op("ve", lambda e: e.tensor_tensor_scan(
            out=S["Gfm"][:, 1:129], data0=S["glogfm"][:], data1=S["glogfm"][:],
            initial=0.0, op0=ALU.add, op1=ALU.bypass), waits={"ve": glg["ve"]})
        gpl = TT("ve", S["gplfm"][:], S["Gfm"][:, 1:129], S["lnbfm"][:], ALU.add)

        def split3(src, dst):
            TS("ve", S["rsq"][:], src, 2.0 ** 29, 2.0 ** 29, ALU.add, ALU.subtract)
            TS("ve", S["t025"][:], src, 2.0 ** 21, 2.0 ** 21, ALU.add, ALU.subtract)
            CP("ve", dst[:, 1:129], S["rsq"][:])
            TT("ve", dst[:, 130:258], S["t025"][:], S["rsq"][:], ALU.subtract)
            return TT("ve", dst[:, 259:387], src, S["t025"][:], ALU.subtract)
        spg = split3(S["Gfm"][:, 1:129], S["splg"])
        spp = split3(S["gplfm"][:], S["splp"])
        fd = None
        for j in range(3):
            fd = op("gp", (lambda e, j=j: e.dma_start(
                out=S["repA"][j:j + 1, :],
                in_=S["splp"][:, 129 * j + 1:129 * j + 129])),
                waits={"ve": spp["ve"]}, inc=16)
            fd = op("gp", (lambda e, j=j: e.dma_start(
                out=S["repM"][j:j + 1, :],
                in_=S["splg"][:, 129 * j + 0:129 * j + 128])),
                waits={"ve": spg["ve"]}, inc=16)
        rep_dma = fd

        tws = {"ve": gpl["ve"], "ac": bsg["ac"]}
        TR(psS[:, 128:144], S["Gfm"][:, 1:129], S["idf32_s"][0:16, 0:16], waits=tws)
        TR(psS[:, 144:160], S["Gfm"][:, 0:128], S["idf32_s"][0:16, 0:16])
        TR(psS[:, 160:176], S["gplfm"][:], S["idf32_s"][0:16, 0:16])
        t4 = TR(psS[:, 176:192], S["bfm"][:], S["idf32_s"][0:16, 0:16])
        gtm_ev = CP("ve", S["gtm"][:], psS[:, 128:144], waits={"te": t4["te"]})
        egp = ACT(S["egprev"][:], psS[:, 144:160], AF.Exp, waits={"te": t4["te"]})
        bge = ACT(S["bgetm"][:], psS[:, 160:176], AF.Exp, waits={"te": t4["te"]})
        btm_ev = CP("ve", S["btm"][:], psS[:, 176:192], waits={"te": t4["te"]})
        g1 = op("dg", lambda e: e.dma_start(out=S["gcrow"][:], in_=S["Gfm"][:, 128:129]),
                waits={"ve": gsc["ve"]}, inc=16)
        g2a = TS("ve", S["gc64"][:], S["gcrow"][:], 2.0 ** 29, 2.0 ** 29, ALU.add,
                 ALU.subtract, waits={"dg": g1["dg"]})
        TS("ve", S["gc025"][:], S["gcrow"][:], 2.0 ** 21, 2.0 ** 21, ALU.add, ALU.subtract)
        CP("ve", S["gA"][:], S["gc64"][:])
        TT("ve", S["gB"][:], S["gc025"][:], S["gc64"][:], ALU.subtract)
        g2b = TT("ve", S["gC"][:], S["gcrow"][:], S["gc025"][:], ALU.subtract)
        op("dg", lambda e: e.dma_start(out=S["grow3"][0:1, :], in_=S["gA"][:]),
           waits={"ve": g2b["ve"]}, inc=16)
        op("dg", lambda e: e.dma_start(out=S["grow3"][1:2, :], in_=S["gB"][:]), inc=16)
        g2c = op("dg", lambda e: e.dma_start(out=S["grow3"][2:3, :], in_=S["gC"][:]),
                 inc=16)
        g3 = MM(psS[:, 208:224], S["ones3_s"][:], S["grow3"][:], True, True,
                waits={"dg": g2c["dg"]})
        ad = TT("ve", S["ab"][:, 0:16], psS[:, 208:224], S["gtm"][:], ALU.subtract,
                waits={"te": g3["te"], "ve": gtm_ev["ve"]})
        exd = ACT(S["expdel"][:], S["ab"][:, 0:16], AF.Exp, waits={"ve": ad["ve"]})
        gm = ACT(S["gam"][:], psS[0:64, 208:224], AF.Exp, waits={"te": g3["te"]})

        if phase <= 2:
            sc_end_ve = cnt["ve"]; sc_end_te = cnt["te"]
            continue
        prev_ev = {"ve": cnt["ve"]}
        sil_d = {}
        ab_guard = exd
        for wn, pn, pi in [("wq", "q", 0), ("wk", "k", 1), ("wv", "v", 2), ("wg", "g", 3)]:
            mm = None
            for ot in range(8):
                for j in range(8):
                    w = None
                    if ot == 0 and j == 0:
                        w = {"ac": ln2["ac"], "ve": prev_ev["ve"]}
                    mm = MM(psB[:, 128 * ot:128 * ot + 128],
                            S[wn + "s"][:, 1024 * j + 128 * ot:1024 * j + 128 * ot + 128],
                            S["lnT"][:, 128 * j:128 * j + 128], j == 0, j == 7, waits=w)
            if pn == "g":
                prev_ev = CP("ve", S["gsil"][:], psB[:], waits={"te": mm["te"]})
                continue
            cvr = S["cvb"][:].rearrange("p (b w) -> p b w", w=131)
            t_new = op("ve", (lambda e, pi=pi: e.tensor_copy(
                out=S["ab"][:, 0:24].rearrange("p (b w) -> p b w", w=3),
                in_=psB[:].rearrange("p (b w) -> p b w", w=128)[:, :, 125:128])),
                waits={"te": mm["te"], "ac": ab_guard["ac"]})
            ev1 = op("ve", (lambda e: e.tensor_copy(
                out=S["cvb"][:].rearrange("p (b w) -> p b w", w=131)[:, :, 3:131],
                in_=psB[:].rearrange("p (b w) -> p b w", w=128))))
            ev2 = op("ve", (lambda e, pi=pi: e.tensor_copy(
                out=S["cvb"][:].rearrange("p (b w) -> p b w", w=131)[:, :, 0:3],
                in_=S["tails"][:, 24 * pi:24 * pi + 24]
                .rearrange("p (b w) -> p b w", w=3))))
            ev3 = op("ve", (lambda e, pi=pi: e.tensor_copy(
                out=S["tails"][:, 24 * pi:24 * pi + 24], in_=S["ab"][:, 0:24])))
            dst = {"q": "qsil", "k": "ksil", "v": "vsil"}[pn]
            st = None
            for bkt in range(8):
                base = 131 * bkt
                cwc = S["cws"][:, 12 * bkt + 4 * pi:12 * bkt + 4 * pi + 4]
                st = TS("ve", S["ab"][:, 128 * bkt:128 * bkt + 128],
                        S["cvb"][:, base + 0:base + 128], cwc[:, 0:1],
                        waits={"ve": ev3["ve"]} if bkt == 0 else None)
                for tap in range(1, 4):
                    st = op("ve", (lambda e, bkt=bkt, tap=tap, base=base, cwc=cwc:
                                   e.scalar_tensor_tensor(
                        out=S["ab"][:, 128 * bkt:128 * bkt + 128],
                        in0=S["cvb"][:, base + tap:base + tap + 128],
                        scalar=cwc[:, tap:tap + 1],
                        in1=S["ab"][:, 128 * bkt:128 * bkt + 128],
                        op0=ALU.mult, op1=ALU.add)))
            sl = None
            for bkt in range(8):
                sl = op("ac", (lambda e, bkt=bkt, pi=pi, dst=dst: e.activation(
                    out=S[dst][:, 128 * bkt:128 * bkt + 128],
                    in_=S["ab"][:, 128 * bkt:128 * bkt + 128], func=AF.Silu,
                    bias=S["cbs"][:, 3 * bkt + pi:3 * bkt + pi + 1])),
                    waits={"ve": st["ve"]} if bkt == 0 else None)
            sil_d[pn] = sl
            ab_guard = sl
            prev_ev = ev1

        if phase <= 3:
            sc_end_ve = cnt["ve"]; sc_end_te = cnt["te"]
            continue
        csl = S["csb"][:, 0:128]
        ssl = S["csb"][:, 128:256]
        csw = {"dcs": last_cs["dcs"]}
        prev_mm_guard = None
        for srcn, dstn in [("qsil", "qsil"), ("ksil", "ksil")]:
            w1 = {"ac": sil_d["k" if srcn == "ksil" else "q"]["ac"]}
            w1.update(csw)
            if prev_mm_guard is not None:
                w1["te"] = prev_mm_guard
            csl4 = csl.unsqueeze(1).broadcast_to([128, 4, 128])
            ssl4 = ssl.unsqueeze(1).broadcast_to([128, 4, 128])
            def v3(ap):
                return ap.rearrange("p (a c) -> p a c", a=4)
            TT("ve", v3(S["X0"][:, 1024:1536]), v3(S[srcn][:, 0:512]), csl4,
               ALU.mult, waits=w1)
            TT("ve", v3(S["X0"][:, 1536:2048]), v3(S[srcn][:, 0:512]), ssl4, ALU.mult)
            TT("ve", v3(S["KKm"][:, 0:512]), v3(S[srcn][:, 512:1024]), ssl4, ALU.mult)
            TT("ve", v3(S["KKm"][:, 512:1024]), v3(S[srcn][:, 512:1024]), csl4, ALU.mult)
            TT("ve", S["X1"][:, 0:512], S["X0"][:, 1024:1536], S["KKm"][:, 0:512],
               ALU.subtract)
            v6 = TT("ve", S["X1"][:, 512:1024], S["X0"][:, 1536:2048],
                    S["KKm"][:, 512:1024], ALU.add)
            sq = ACT(S["X0"][:, 0:512], S["X1"][:, 0:512], AF.Square,
                     waits={"ve": v6["ve"]})
            sq = ACT(S["X0"][:, 512:1024], S["X1"][:, 512:1024], AF.Square)
            sqmm = None
            for j in range(8):
                sqmm = MM(psS[0:16, 224:352], S["bo4b_s"][:, 16 * (j % 4):16 * (j % 4) + 16],
                          S["X0"][:, 128 * j:128 * j + 128], j == 0, j == 7,
                          waits={"ac": sq["ac"]} if j == 0 else None)
            rq2 = ACT(S["sp2"][0:16, :], psS[0:16, 224:352], AF.Ln, bias=1e-12,
                      waits={"te": sqmm["te"]})
            rq3 = ACT(S["rsq"][:], S["sp2"][0:16, :], AF.Exp, scale=-0.5)
            rq4 = CP("ve", S["rsqb"][:], S["rsq"][:], waits={"ac": rq3["ac"]})
            rp = None
            for pr in range(4):
                rp = MM(psA[:, 128 * pr:128 * pr + 128],
                        S["dn4_s"][:, 128 * pr:128 * pr + 128],
                        S["rsqb"][:], True, True,
                        waits={"ve": rq4["ve"]} if pr == 0 else None)
            rc = op("ac", lambda e: e.copy(out=S["E1"][:, 0:512], in_=psA[:, 0:512]),
                    waits={"te": rp["te"]})
            TT("ve", S[dstn][:, 0:512], S["X1"][:, 0:512], S["E1"][:, 0:512],
               ALU.mult, waits={"ac": rc["ac"]})
            lsq = TT("ve", S[dstn][:, 512:1024], S["X1"][:, 512:1024],
                     S["E1"][:, 0:512], ALU.mult)
            prev_mm_guard = rp["te"]
        ql2 = lsq

        if c + 1 < nsc:
            last_cs = op("dcs", (lambda e, c=c: e.dma_start(
                out=S["csb"][:, 0:128],
                in_=cs[:, 128 * (c + 1):128 * (c + 1) + 128])),
                waits={"ve": ql2["ve"]}, inc=16)
            last_cs = op("dcs", (lambda e, c=c: e.dma_start(
                out=S["csb"][:, 128:256],
                in_=cs[:, T + 128 * (c + 1):T + 128 * (c + 1) + 128])), inc=16)

        if phase <= 4:
            sc_end_ve = cnt["ve"]; sc_end_te = cnt["te"]
            continue
        evk = None
        for g in range(4):
            src = S["ksil"] if g < 2 else S["vsil"]
            dstn = "ktm" if g < 2 else "vtm"
            gg = g % 2
            tp = None
            for j in range(4):
                jj = 4 * gg + j
                w = {"ve": ql2["ve"]} if g == 0 and j == 0 else None
                if evk and j == 0:
                    w = {"ac": evk["ac"]}
                tp = TR(psT[:, 128 * j:128 * j + 128], src[:, 128 * jj:128 * jj + 128],
                        S["idbf_s"][:], waits=w)
            evk = op("ac", (lambda e, dstn=dstn, gg=gg: e.copy(
                out=S[dstn][:, 512 * gg:512 * gg + 512], in_=psT[:])),
                waits={"te": tp["te"]})

        if phase <= 4.2:
            sc_end_ve = cnt["ve"]; sc_end_te = cnt["te"]
            continue
        kd = None
        for (srcn, dstn, rr) in [("ksil", "X0", 0), ("qsil", "X1", 0)]:
            for a in range(4):
                for eo in range(2):
                    w = {"ve": ql2["ve"]} if kd is None else None
                    kd = op("dk", (lambda e, srcn=srcn, dstn=dstn, a=a, eo=eo: e.dma_start(
                        out=S[dstn][32 * eo:32 * eo + 32, :]
                        .rearrange("p (m c) -> p m c", m=16)[:, a::4, :],
                        in_=S[srcn][32 * a:32 * a + 32, 512 * eo:512 * eo + 512]
                        .rearrange("p (b c) -> p b c", b=4))), waits=w, inc=16)
        krs_dma = kd

        mmr = None
        for seg in range(4):
            mmr = MM(psA[:, 512 * seg:512 * seg + 512], S["ones3_s"][:],
                     S["repA"][:, 512 * seg:512 * seg + 512], True, True,
                     waits={"gp": rep_dma["gp"], "ac": evk["ac"]} if seg == 0 else None)
        ats = None
        for h in range(H):
            ats = TS("ve", S["arg1"][:, 128 * h:128 * h + 128],
                     psA[:, 128 * h:128 * h + 128], S["gtm"][:, h:h + 1], 0.0,
                     ALU.subtract, ALU.min,
                     waits={"te": mmr["te"], "ve": gtm_ev["ve"]} if h == 0 else None)
        eA = ACT(S["E1"][:], S["arg1"][:], AF.Exp, waits={"ve": ats["ve"]})
        if phase <= 4.4:
            sc_end_ve = cnt["ve"]; sc_end_te = cnt["te"]
            continue
        mmk = None
        for h in range(H):
            kS = S["X0"][0:64, 128 * h:128 * h + 128]
            w = {"ve": ats["ve"], "dk": krs_dma["dk"]} if h == 0 else None
            mmk = MM(psA[:, 128 * h:128 * h + 128], kS, kS, True, True, waits=w)
        if phase <= 4.5:
            sc_end_ve = cnt["ve"]; sc_end_te = cnt["te"]
            continue
        kkev = op("ve", lambda e: e.tensor_tensor(
            out=S["KKm"][:].rearrange("p (h t) -> p h t", h=16),
            in0=psA[:].rearrange("p (h t) -> p h t", h=16),
            in1=S["mneg"][:].unsqueeze(1).broadcast_to([128, 16, 128]),
            op=ALU.mult), waits={"te": mmk["te"]})
        nt0 = TT("ve", S["NT0"][:], S["E1"][:], S["KKm"][:], ALU.mult,
                 waits={"ac": eA["ac"]})
        if phase <= 4.6:
            sc_end_ve = cnt["ve"]; sc_end_te = cnt["te"]
            continue
        mmr = None
        for seg in range(4):
            mmr = MM(psA[:, 512 * seg:512 * seg + 512], S["ones3_s"][:],
                     S["repM"][:, 512 * seg:512 * seg + 512], True, True,
                     waits={"ve": kkev["ve"]} if seg == 0 else None)
        ats = None
        for h in range(H):
            ats = TS("ve", S["arg1"][:, 128 * h:128 * h + 128],
                     psA[:, 128 * h:128 * h + 128], S["gtm"][:, h:h + 1], 0.0,
                     ALU.subtract, ALU.min, waits={"te": mmr["te"]} if h == 0 else None)
        eM = ACT(S["E1"][:], S["arg1"][:], AF.Exp, waits={"ve": ats["ve"]})
        mmk = None
        for h in range(H):
            kS = S["X0"][0:64, 128 * h:128 * h + 128]
            qS = S["X1"][0:64, 128 * h:128 * h + 128]
            w = {"ve": ats["ve"]} if h == 0 else None
            mmk = MM(psA[:, 128 * h:128 * h + 128], kS, qS, True, True, waits=w)
        if phase <= 4.8:
            sc_end_ve = cnt["ve"]; sc_end_te = cnt["te"]
            continue
        kqev = op("ve", lambda e: e.tensor_tensor(
            out=S["KKm"][:].rearrange("p (h t) -> p h t", h=16),
            in0=psA[:].rearrange("p (h t) -> p h t", h=16),
            in1=S["mpos"][:].unsqueeze(1).broadcast_to([128, 16, 128]),
            op=ALU.mult), waits={"te": mmk["te"]})
        mtd = TT("ve", S["MT"][:], S["E1"][:], S["KKm"][:], ALU.mult,
                 waits={"ac": eM["ac"]})

        if phase <= 5:
            sc_end_ve = cnt["ve"]; sc_end_te = cnt["te"]
            continue
        xb1 = op("ve", lambda e: e.tensor_tensor(
            out=S["X0"][:].rearrange("p (h z) -> p h z", h=16)[:, :, 0:64],
            in0=S["vtm"][:].rearrange("p (h d) -> p h d", h=16),
            in1=S["btm"][:].unsqueeze(-1).broadcast_to([128, 16, 64]),
            op=ALU.mult), waits={"ve": mtd["ve"], "ac": evk["ac"]})
        xb2 = op("ve", lambda e: e.tensor_tensor(
            out=S["X0"][:].rearrange("p (b a z) -> p b a z", b=4, a=4)[:, :, :, 64:128]
            .rearrange("p b a (eo d) -> p b a eo d", eo=2),
            in0=S["ktm"][:].rearrange("p (eo b a d) -> p b a eo d", eo=2, b=4, a=4),
            in1=S["bgetm"][:].rearrange("p (b a) -> p b a", b=4)
            .unsqueeze(-1).unsqueeze(-1).broadcast_to([128, 4, 4, 2, 32]),
            op=ALU.mult))
        kp1 = op("ve", lambda e: e.tensor_tensor(
            out=S["Kp"][:].rearrange("p (b a z) -> p b a z", b=4, a=4)
            .rearrange("p b a (eo d) -> p b a eo d", eo=2),
            in0=S["ktm"][:].rearrange("p (eo b a d) -> p b a eo d", eo=2, b=4, a=4),
            in1=S["expdel"][:].rearrange("p (b a) -> p b a", b=4)
            .unsqueeze(-1).unsqueeze(-1).broadcast_to([128, 4, 4, 2, 32]),
            op=ALU.mult), waits={"ac": exd["ac"]})

        evn = None
        for g in range(4):
            tp = None
            for j in range(4):
                jj = 4 * g + j
                w = {"ve": nt0["ve"]} if g == 0 and j == 0 else None
                if evn and j == 0:
                    w = {"ac": evn["ac"]}
                tp = TR(psT[:, 128 * j:128 * j + 128],
                        S["NT0"][:, 128 * jj:128 * jj + 128], S["idbf_s"][:], waits=w)
            evn = op("ac", (lambda e, g=g: e.copy(
                out=S["NU0"][:, 512 * g:512 * g + 512], in_=psT[:])),
                waits={"te": tp["te"]})

        NTc, NTn = "NT0", "NT1"
        Xc, Xn = "X0", "X1"
        last_ac = evn
        last_ve = {"ve": max(xb1["ve"], xb2["ve"], kp1["ve"])}
        def halves(src_a, src_b_name, dst_fn, evac_eng, wait0, alu_or_copy):
            mms = [None, None]
            evs = [None, None]
            for half in range(2):
                for i in range(8):
                    h = 8 * half + i
                    w = None
                    if half == 0 and i == 0:
                        w = dict(wait0)
                    if half == 1 and i == 0 and evs[0] is None:
                        w = None
                    mms[half] = MM(psA[:, 128 * h:128 * h + 128],
                                   src_a(h), S[src_b_name][:, 128 * h:128 * h + 128],
                                   True, True, waits=w)
                evs[half] = dst_fn(half, mms[half])
            return evs[1]
        for lvl in range(7):
            w0l = {"ac": last_ac["ac"], "ve": last_ve["ve"]}
            mmA = [None, None]
            for half in range(2):
                for i in range(8):
                    h = 8 * half + i
                    w = dict(w0l) if (half == 0 and i == 0) else None
                    if half == 1 and i == 0:
                        w = None
                    mmA[half] = MM(psA[:, 128 * h:128 * h + 128],
                                   S[NTc][:, 128 * h:128 * h + 128],
                                   S[Xc][:, 128 * h:128 * h + 128], True, True, waits=w)
            x1e = TT("ve", S[Xn][:, 0:1024], psA[:, 0:1024], S[Xc][:, 0:1024], ALU.add,
                     waits={"te": mmA[0]["te"]})
            xev = TT("ve", S[Xn][:, 1024:2048], psA[:, 1024:2048], S[Xc][:, 1024:2048],
                     ALU.add, waits={"te": mmA[1]["te"]})
            last_ve = xev
            if lvl < 6:
                mmB = [None, None]
                for half in range(2):
                    for i in range(8):
                        h = 8 * half + i
                        w = None
                        if half == 0 and i == 0:
                            w = {"ve": x1e["ve"]}
                        if half == 1 and i == 0:
                            w = {"ve": xev["ve"]}
                        mmB[half] = MM(psA[:, 128 * h:128 * h + 128],
                                       S["NU0"][:, 128 * h:128 * h + 128],
                                       S[NTc][:, 128 * h:128 * h + 128], True, True,
                                       waits=w)
                n1 = op("ac", (lambda e, NTn=NTn: e.copy(out=S[NTn][:, 0:1024],
                                                         in_=psA[:, 0:1024])),
                        waits={"te": mmB[0]["te"]})
                ntc = op("ac", (lambda e, NTn=NTn: e.copy(out=S[NTn][:, 1024:2048],
                                                          in_=psA[:, 1024:2048])),
                        waits={"te": mmB[1]["te"]})
                last_ac = ntc
                if lvl < 5:
                    mmC = [None, None]
                    for half in range(2):
                        for i in range(8):
                            h = 8 * half + i
                            w = None
                            if half == 0 and i == 0:
                                w = {"ac": n1["ac"]}
                            if half == 1 and i == 0:
                                w = {"ac": ntc["ac"]}
                            mmC[half] = MM(psA[:, 128 * h:128 * h + 128],
                                           S[NTc][:, 128 * h:128 * h + 128],
                                           S["NU0"][:, 128 * h:128 * h + 128], True,
                                           True, waits=w)
                    u1 = op("ac", lambda e: e.copy(out=S["NU0"][:, 0:1024],
                                                   in_=psA[:, 0:1024]),
                            waits={"te": mmC[0]["te"]})
                    evn2 = op("ac", lambda e: e.copy(out=S["NU0"][:, 1024:2048],
                                                     in_=psA[:, 1024:2048]),
                             waits={"te": mmC[1]["te"]})
                    last_ac = evn2
            NTc, NTn = NTn, NTc
            Xc, Xn = Xn, Xc
        Xf = Xc

        if phase <= 6:
            sc_end_ve = cnt["ve"]; sc_end_te = cnt["te"]
            continue
        mmw = None
        for h in range(H):
            w = {"ve": last_ve["ve"]} if h == 0 else None
            mmw = MM(psA[0:64, 128 * h:128 * h + 128],
                     S[Xf][:, 128 * h + 64:128 * h + 128],
                     S["MT"][:, 128 * h:128 * h + 128], True, True, waits=w)
        nmk = TS("ve", S["arg1"][0:64, :], psA[0:64, :], -1.0, waits={"te": mmw["te"]})
        mmw = None
        for h in range(H):
            w = {"ve": nmk["ve"]} if h == 0 else None
            mmw = MM(psB[0:64, 64 * h:64 * h + 64],
                     S[Xf][:, 128 * h + 64:128 * h + 128],
                     S["Kp"][:, 64 * h:64 * h + 64], True, True, waits=w)
        nuk = TS("ve", S["E1"][0:64, 0:1024], psB[0:64, :], -1.0, waits={"te": mmw["te"]})

        mq = None
        for h in range(H):
            a, b2 = h % 4, h // 4
            w = {"ve": nuk["ve"]} if h == 0 else None
            mq = MM(psA[:, 128 * h:128 * h + 64], S["qsil"][:, 128 * b2:128 * b2 + 128],
                    S["SbE"][:, 64 * h:64 * h + 64], True, False, waits=w)
            mq = MM(psA[:, 128 * h:128 * h + 64],
                    S["qsil"][:, 512 + 128 * b2:512 + 128 * b2 + 128],
                    S["SbO"][:, 64 * h:64 * h + 64], False, True)
        mf = None
        for h in range(H):
            mf = MM(psB[:, 64 * h:64 * h + 64], S["arg1"][0:64, :][:, 128 * h:128 * h + 128],
                    S["Sb"][:, 64 * h:64 * h + 64], True, False)
            mf = MM(psB[:, 64 * h:64 * h + 64], S["MT"][:, 128 * h:128 * h + 128],
                    S[Xf][:, 128 * h:128 * h + 64], False, True)
        fmev = op("ac", lambda e: e.copy(out=S["ab"][:], in_=psB[:]),
                  waits={"te": mf["te"]})
        egb = S["egprev"][:].unsqueeze(-1).broadcast_to([128, 16, 64])
        qpart = psA[:].rearrange("p (h z) -> p h z", h=16)[:, :, 0:64]
        ot1 = op("ve", (lambda e: e.tensor_tensor(
            out=S["X1"][:, 0:1024].bitcast(F32)[:, 0:512]
            .rearrange("p (h z) -> p h z", h=16) if False else
            S["KKm"][:].bitcast(F32)[:, 0:1024].rearrange("p (h z) -> p h z", h=16)[:, :, 0:64],
            in0=qpart, in1=egb, op=ALU.mult)),
            waits={"ac": fmev["ac"], "te": mq["te"]})
        oev = op("ve", (lambda e: e.tensor_tensor(
            out=S["cvb"][:, 0:1024],
            in0=S["KKm"][:].bitcast(F32)[:, 0:1024],
            in1=S["ab"][:], op=ALU.add)))


        mb = None
        for h in range(H):
            w = {"ve": oev["ve"]} if h == 0 else None
            mb = MM(psB[0:64, 64 * h:64 * h + 64], S["E1"][0:64, 0:1024][:, 64 * h:64 * h + 64],
                    S["Sb"][:, 64 * h:64 * h + 64], True, False, waits=w)
            mb = MM(psB[0:64, 64 * h:64 * h + 64], S["Kp"][:, 64 * h:64 * h + 64],
                    S[Xf][:, 128 * h:128 * h + 64], False, True)
        gmb = S["gam"][:].unsqueeze(-1).broadcast_to([64, 16, 64])
        op("ve", (lambda e: e.tensor_tensor(
            out=S["Sf"][:].rearrange("p (h z) -> p h z", h=16),
            in0=S["Sf"][:].rearrange("p (h z) -> p h z", h=16),
            in1=gmb, op=ALU.mult)), waits={"te": mb["te"], "ac": gm["ac"]})
        sup = op("ve", (lambda e: e.tensor_tensor(
            out=S["Sf"][:], in0=S["Sf"][:], in1=psB[0:64, :], op=ALU.add)))
        sbc = CP("ve", S["Sb"][:], S["Sf"][:], waits={"ve": sup["ve"]})
        pad = None
        for r in range(4):
            pad = op("ve", (lambda e, r=r: e.tensor_copy(
                out=S["SbE"][32 * r:32 * r + 32, :]
                .rearrange("p (m q) -> p m q", m=4)[:, :, 64 * r:64 * r + 64],
                in_=S["Sb"][0:32, :]
                .rearrange("p (m q) -> p m q", m=4)[:, :, 64 * r:64 * r + 64])))
            pad = op("ve", (lambda e, r=r: e.tensor_copy(
                out=S["SbO"][32 * r:32 * r + 32, :]
                .rearrange("p (m q) -> p m q", m=4)[:, :, 64 * r:64 * r + 64],
                in_=S["Sb"][32:64, :]
                .rearrange("p (m q) -> p m q", m=4)[:, :, 64 * r:64 * r + 64])))

        if phase <= 7:
            sc_end_ve = cnt["ve"]; sc_end_te = cnt["te"]
            continue
        prod = None
        for pr in range(8):
            prod = TT("ve", S["KKm"][:, 128 * pr:128 * pr + 128],
                      S["qsil"][:, 128 * pr:128 * pr + 128],
                      S["ksil"][:, 128 * pr:128 * pr + 128], ALU.mult)
        mmd = None
        for pr in range(8):
            mmd = MM(psS[0:16, 224:352], S["bo4dp"][:, 16 * (pr % 4):16 * (pr % 4) + 16],
                     S["KKm"][:, 128 * pr:128 * pr + 128], pr == 0, pr == 7,
                     waits={"ve": prod["ve"]} if pr == 0 else None)
        qk1 = CP("ve", S["rsq"][:], psS[0:16, 224:352], waits={"te": mmd["te"]})
        qtr = TR(psS[:, 352:368], S["rsq"][:], S["idf32_s"][0:16, 0:16], waits={"ve": qk1["ve"]})
        qk2 = CP("ve", S["qkT"][:], psS[:, 352:368], waits={"te": qtr["te"]})
        qkb = S["qkT"][:].unsqueeze(-1).broadcast_to([128, 16, 64])
        op("ve", (lambda e: e.tensor_tensor(
            out=S["ab"][:].rearrange("p (h z) -> p h z", h=16),
            in0=S["vtm"][:].rearrange("p (h z) -> p h z", h=16),
            in1=qkb, op=ALU.mult)), waits={"ve": qk2["ve"]})
        dpv = TT("ve", S["cvb"][:, 0:1024], S["cvb"][:, 0:1024], S["ab"][:], ALU.add)

        sqo = ACT(S["ab"][:], S["cvb"][:, 0:1024][:], AF.Square, waits={"ve": dpv["ve"]})
        rs1 = op("ve", lambda e: e.tensor_reduce(
            out=S["egprev"][:], in_=S["ab"][:].rearrange("p (h d) -> p h d", h=16),
            axis=AXX, op=ALU.add), waits={"ac": sqo["ac"]})
        rs2 = ACT(S["qkT"][:], S["egprev"][:], AF.Ln, bias=1e-6, scale=1.0 / 64.0,
                  waits={"ve": rs1["ve"]})
        rs3x = ACT(S["egprev"][:], S["qkT"][:], AF.Exp, scale=-0.5)
        rsb = S["egprev"][:].unsqueeze(-1).broadcast_to([128, 16, 64])
        orv = op("ve", (lambda e: e.tensor_tensor(
            out=S["qsil"][:, 0:1024].rearrange("p (h z) -> p h z", h=16),
            in0=S["cvb"][:, 0:1024].rearrange("p (h z) -> p h z", h=16),
            in1=rsb, op=ALU.mult)), waits={"ac": rs3x["ac"]})
        evo = None
        for g in range(2):
            tp = None
            for j in range(4):
                jj = 4 * g + j
                w = {"ve": orv["ve"]} if g == 0 and j == 0 else None
                if evo and j == 0:
                    w = {"ac": evo["ac"]}
                tp = TR(psT[:, 128 * j:128 * j + 128], S["qsil"][:, 128 * jj:128 * jj + 128],
                        S["idbf_s"][:], waits=w)
            evo = op("ac", (lambda e, g=g: e.copy(
                out=S["X0"][:, 512 * g:512 * g + 512], in_=psT[:])),
                waits={"te": tp["te"]})
        slw = ACT(S["X1"][:, 0:1024], S["X0"][:, 0:1024], AF.Silu,
                  scale=S["onw2"][:, 0:1], waits={"ac": evo["ac"]})
        ofd = TT("ve", S["ksil"][:], S["X1"][:, 0:1024], S["gsil"][:], ALU.mult,
                 waits={"ac": slw["ac"]})

        mmo = None
        for j in range(8):
            w = {"ve": ofd["ve"]} if j == 0 else None
            mmo = MM(psB[:, 0:512], S["ksil"][:, 128 * j:128 * j + 128],
                     S["wos"][:, 1024 * j:1024 * j + 512], j == 0, j == 7, waits=w)
            mmo = MM(psB[:, 512:1024], S["ksil"][:, 128 * j:128 * j + 128],
                     S["wos"][:, 1024 * j + 512:1024 * j + 1024], j == 0, j == 7)
        x2d = TT("ve", S["x2t"][:], psB[:], S["xt"][:], ALU.add, waits={"te": mmo["te"]})

        if phase <= 8:
            sc_end_ve = cnt["ve"]; sc_end_te = cnt["te"]
            continue
        ln3 = ln(S["x2t"][:], {"ve": x2d["ve"]})
        yd = ffn(S["x2t"][:], S["xt"][:], ln3)
        od = op("dy", (lambda e, c=c: e.dma_start(
            out=y_out[128 * c:128 * c + 128, :], in_=S["xt"][:])),
            waits={"ve": yd["ve"]}, inc=16)
        last_od = od
        sc_end_ve = yd["ve"]
        sc_end_te = cnt["te"]

    final_dy = cnt["dy"]
    op("dma", (lambda e: e.nop()), waits={"dy": final_dy}, inc=0)

    with nc.Block() as blk:
        engmap = {"te": blk.tensor, "ve": blk.vector, "ac": blk.scalar,
                  "dma": blk.sync, "gp": blk.gpsimd}

        def make(ek):
            def run(e):
                for (w, fn, inc, sk, pre) in sched[ek]:
                    if sk in ("ve", "ac") and pre > 0:
                        e.wait_ge(sems[sk], pre)
                    for k2, v in w.items():
                        e.wait_ge(sems[k2], v)
                    ins = fn(e)
                    if inc:
                        ins.then_inc(sems[sk], inc)
            return run
        for ek in ["te", "ve", "ac", "dma", "gp"]:
            engmap[ek](make(ek))

    for h in reversed(hold):
        h.__exit__(None, None, None)
    return nc


def _host_inputs(inp):
    f32 = np.float32
    A = {k[8:]: np.asarray(v, f32) for k, v in inp.items() if k.startswith("L0_attn_")}
    F = {k[7:]: np.asarray(v, f32) for k, v in inp.items() if k.startswith("L0_ffn_")}
    perm = np.empty(1024, np.int64)
    for r in range(1024):
        eo = r // 512
        rr = r % 512
        a, p = rr // 128, rr % 128
        h, j = 4 * a + p // 32, p % 32
        perm[r] = 64 * h + 2 * j + eo
    d = {}
    d["wq"] = np.ascontiguousarray(A["Wq"][perm].T)
    d["wk"] = np.ascontiguousarray(A["Wk"][perm].T)
    d["wv"] = np.ascontiguousarray(A["Wv"].T)
    d["wg"] = np.ascontiguousarray(A["Wg"].T)
    d["wo"] = np.ascontiguousarray(A["Wo"].T)
    d["fw"] = np.ascontiguousarray(F["weight"].T)
    U2 = F["lora_U"].reshape(128, 1024)
    d["flo"] = np.ascontiguousarray(np.concatenate([U2.T, F["gate_w"].T], axis=1))
    d["fv2"] = np.ascontiguousarray(np.moveaxis(F["lora_V"], 2, 1).reshape(128, 1024))
    d["wba"] = np.ascontiguousarray(np.concatenate(
        [A["Wgk"].T, np.zeros((1024, 16), f32), A["Wb"].T], axis=1))
    cwq = A["cqw"][perm, 0]; cwk = A["ckw"][perm, 0]; cwv = A["cvw"][:, 0]
    cbq = A["cqb"][perm]; cbk = A["ckb"][perm]; cbv = A["cvb"]
    cwh = np.zeros((128, 96), f32)
    cbh = np.zeros((128, 24), f32)
    for bkt in range(8):
        for pi, (wv_, bv_) in enumerate([(cwq, cbq), (cwk, cbk), (cwv, cbv)]):
            cwh[:, 12 * bkt + 4 * pi:12 * bkt + 4 * pi + 4] = wv_[128 * bkt:128 * bkt + 128]
            cbh[:, 3 * bkt + pi] = bv_[128 * bkt:128 * bkt + 128]
    d["cw"] = cwh; d["cb"] = cbh
    d["smalls"] = np.ascontiguousarray(np.stack(
        [A["bgk"], A["dt_bias"], A["bb"], A["A_log"], A["Dp"]], axis=1))
    d["onw"] = np.ascontiguousarray(A["onorm_w"][:, None])
    st = np.triu(np.ones((128, 128), f32), 1)
    d["maskneg"] = -st
    d["maskpos"] = st
    dn = np.zeros((16, 512), NPBF)
    for a in range(4):
        for p in range(128):
            dn[4 * a + p // 32, 128 * a + p] = 1
    d["dn4"] = dn
    bo4t = np.ascontiguousarray(dn.astype(f32).reshape(16, 4, 128)
                                .transpose(2, 1, 0).reshape(128, 64))
    d["bo4f"] = bo4t
    d["bo4b"] = bo4t.astype(NPBF)
    d["ones3"] = np.ones((3, 128), NPBF)
    d["ones1f"] = np.ones((1, 128), f32)
    d["eye16f"] = np.eye(16, dtype=f32)
    d["idbf"] = np.eye(128, dtype=NPBF)
    d["idf32"] = np.eye(128, dtype=f32)
    invf = (1.0 / 10000.0 ** (np.arange(0, D, 2, dtype=f32) / D))
    tt = np.arange(T, dtype=f32)
    j32 = (2 * (np.arange(128) % 32)) % 32
    ang = np.outer(invf[j32], tt)
    d["cs"] = np.ascontiguousarray(
        np.concatenate([np.cos(ang), np.sin(ang)], axis=1).astype(NPBF))
    return d


def kernel(**inputs):
    shared = _host_inputs(inputs)
    x = np.asarray(inputs["L0_attn_x"], np.float32)
    if "nc" not in _cache:
        _cache["nc"] = build_nc()
    nc = _cache["nc"]
    in_maps = []
    for b in range(8):
        m = dict(shared)
        m["x"] = np.ascontiguousarray(x[b])
        in_maps.append(m)
    res = run_bass_kernel_spmd(nc, in_maps, core_ids=list(range(8)))
    out = np.stack([r["y"] for r in res.results], axis=0)
    return out.astype(np.float32)


# version 13 (final)
# speedup vs baseline: 1.1534x; 1.0230x over previous
import numpy as np
import ml_dtypes
import concourse.bass as bass
import concourse.mybir as mybir
from concourse.bass_utils import run_bass_kernel_spmd

F32 = mybir.dt.float32
BF16 = mybir.dt.bfloat16
AF = mybir.ActivationFunctionType
ALU = mybir.AluOpType
AXX = mybir.AxisListType.X
NPBF = ml_dtypes.bfloat16

T, HID, H, D, C = 4096, 1024, 16, 64, 128
NSC = T // C
_cache = {}


def build_nc(nsc=NSC, phase=9.0):
    nc = bass.Bass("TRN2", target_bir_lowering=False)

    def din(name, shape, dtype=F32):
        return nc.dram_tensor(name, list(shape), dtype, kind="ExternalInput").ap()

    x_in = din("x", (T, HID))
    y_out = nc.dram_tensor("y", [T, HID], F32, kind="ExternalOutput").ap()
    W_dram = {n: din(n, (HID, HID)) for n in ["wq", "wk", "wv", "wg", "wo", "fw"]}
    flo = din("flo", (HID, 136)); fv2 = din("fv2", (128, HID))
    wba = din("wba", (HID, 48))
    cw = din("cw", (128, 96)); cb = din("cb", (128, 24))
    smalls = din("smalls", (16, 5)); onw = din("onw", (64, 1))
    maskneg = din("maskneg", (128, 128)); maskpos = din("maskpos", (128, 128))
    dn4 = din("dn4", (16, 512), BF16)
    bo4dpf = din("bo4f", (128, 64))
    bo4b = din("bo4b", (128, 64), BF16)
    ones3 = din("ones3", (3, 128), BF16)
    ones1f = din("ones1f", (1, 128))
    eye16f = din("eye16f", (16, 16))
    idbf = din("idbf", (128, 128), BF16)
    idf32 = din("idf32", (128, 128))
    cs = din("cs", (128, 2 * T), BF16)

    S = {}
    hold = []

    def sb(name, shape, dtype):
        h = nc.sbuf_tensor(name, list(shape), dtype)
        hold.append(h)
        S[name] = h.__enter__()

    for n in ["wq", "wk", "wv", "wg", "wo", "fw"]:
        sb(n + "s", (128, 8192), BF16)
    sb("flos", (128, 8 * 136), BF16)
    sb("fv2s", (128, 1024), BF16)
    sb("wbas", (128, 8 * 48), BF16)
    sb("cws", (128, 96), F32); sb("cbs", (128, 24), F32)
    sb("smalls_s", (16, 5), F32); sb("onw2", (128, 1), F32)
    sb("mneg", (128, 128), F32); sb("mpos", (128, 128), F32)
    sb("dn4_s", (16, 512), BF16); sb("bo4dp", (128, 64), BF16)
    sb("bo4f_s", (128, 64), F32); sb("bo4b_s", (128, 64), BF16)
    sb("ones3_s", (3, 128), BF16)
    sb("idbf_s", (128, 128), BF16); sb("idf32_s", (128, 128), F32)
    sb("stage", (128, 512), F32)
    sb("constv", (128, 8), F32)
    sb("constvb", (128, 4), BF16)
    sb("bias48", (48, 1), F32); sb("negea", (16, 1), F32)
    sb("csb", (128, 256), BF16)
    sb("xin", (128, 1024), F32)
    sb("xt", (128, 1024), F32)
    sb("x2t", (128, 1024), F32)
    sb("lnt", (128, 1024), BF16)
    sb("lnT", (128, 1024), BF16)
    sb("cvb", (128, 8 * 131), F32)
    sb("ab", (128, 1024), F32)
    sb("tails", (128, 72), F32)
    sb("qsil", (128, 1024), BF16); sb("ksil", (128, 1024), BF16)
    sb("vsil", (128, 1024), BF16); sb("gsil", (128, 1024), BF16)
    sb("ktm", (128, 1024), BF16); sb("vtm", (128, 1024), BF16)
    sb("tb", (128, 128), BF16)
    sb("gates", (128, 8), F32)
    sb("r0", (128, 128), BF16)
    sb("rw", (128, 8), F32)
    sb("rsq", (16, 128), F32); sb("rsqb", (16, 128), BF16)
    sb("spall", (48, 128), F32); sb("sp2", (48, 128), F32)
    sb("bfm", (16, 128), F32); sb("lnbfm", (16, 128), F32)
    sb("glogfm", (16, 128), F32)
    sb("Gfm", (16, 129), F32); sb("gplfm", (16, 128), F32)
    sb("t025", (16, 128), F32)
    sb("splg", (16, 3 * 129), BF16); sb("splp", (16, 3 * 129), BF16)
    sb("repA", (3, 2048), BF16); sb("repM", (3, 2048), BF16)
    sb("gtm", (128, 16), F32)
    sb("egprev", (128, 16), F32)
    sb("expdel", (128, 16), BF16)
    sb("btm", (128, 16), BF16); sb("bgetm", (128, 16), BF16)
    sb("gcrow", (1, 16), F32); sb("gam", (64, 16), F32)
    sb("dpb", (16, 1), BF16); sb("dprowb", (1, 16), BF16)
    sb("gc64", (1, 16), F32); sb("gc025", (1, 16), F32)
    sb("gA", (1, 16), BF16); sb("gB", (1, 16), BF16); sb("gC", (1, 16), BF16)
    sb("grow3", (3, 16), BF16)
    sb("arg1", (128, 2048), BF16); sb("E1", (128, 2048), BF16)
    sb("KKm", (128, 2048), BF16)
    sb("MT", (128, 2048), BF16)
    sb("NT0", (128, 2048), BF16); sb("NT1", (128, 2048), BF16)
    sb("NU0", (128, 2048), BF16)
    sb("X0", (128, 2048), BF16); sb("X1", (128, 2048), BF16)
    sb("Kp", (128, 1024), BF16)
    sb("Sf", (64, 1024), F32)
    sb("Sb", (64, 1024), BF16)
    sb("SbE", (128, 1024), BF16); sb("SbO", (128, 1024), BF16)
    sb("qkT", (128, 16), F32)

    def psb(name, shape, dtype):
        h = nc.psum_tensor(name, list(shape), dtype)
        hold.append(h)
        S[name] = h.__enter__()

    psb("psA", (128, 2048), F32)
    psb("psB", (128, 1024), F32)
    psb("psT", (128, 512), BF16)
    psb("psS", (128, 496), F32)
    psA, psB, psT, psS = S["psA"], S["psB"], S["psT"], S["psS"]

    sems = {}
    for k in ["dma", "dw", "dx", "dy", "dcs", "dg", "dk", "gp", "te", "ve", "ac"]:
        h = nc.semaphore(k)
        hold.append(h)
        sems[k] = h.__enter__()
    cnt = {k: 0 for k in sems}
    qmap = {"dw": "dma", "dx": "dma", "dy": "dma", "dcs": "gp", "dg": "gp", "dk": "gp"}
    sched = {k: [] for k in ["dma", "gp", "te", "ve", "ac"]}

    def op(ek, fn, waits=None, inc=1):
        w = {k: v for k, v in (waits or {}).items() if v and v > 0}
        pre = cnt[ek]
        cnt[ek] += inc
        sched[qmap.get(ek, ek)].append((w, fn, inc, ek, pre))
        return dict(cnt)

    _cvals = [0.0, 1.0, -1.0, 1e-5, 1e-6, 1.0 / 64.0, 1e-12, -0.5]
    for i, v in enumerate(_cvals):
        nc.const_aps.register(F32, v, S["constv"][:, i:i + 1])
    for i, v in enumerate([0.0, 1.0, -1.0]):
        nc.const_aps.register(BF16, v, S["constvb"][:, i:i + 1])

    def f_constv(e):
        for i, v in enumerate(_cvals):
            e.memset(S["constv"][:, i:i + 1], v)
        for i, v in enumerate([0.0, 1.0, -1.0]):
            e.memset(S["constvb"][:, i:i + 1], v)
        return e.memset(S["rw"][:], 0.0)
    op("ve", f_constv)

    d1 = op("dma", lambda e: e.dma_start(out=S["onw2"][0:64, :], in_=onw), inc=16)
    d2 = op("dma", lambda e: e.dma_start(out=S["onw2"][64:128, :], in_=onw), inc=16)
    const_specs = [("cws", cw), ("cbs", cb), ("smalls_s", smalls), ("mneg", maskneg),
                   ("mpos", maskpos), ("dn4_s", dn4), ("bo4f_s", bo4dpf), ("bo4b_s", bo4b), ("ones3_s", ones3),
                   ("idbf_s", idbf),
                   ("idf32_s", idf32)]
    for nm, src in const_specs:
        op("dma", (lambda e, nm=nm, src=src: e.dma_start(out=S[nm][:], in_=src)), inc=16)
    consts_done = dict(cnt)

    prev_cast = None
    for wi, n in enumerate(["fw", "wq", "wk", "wv", "wg", "wo"]):
        for j in range(8):
            for hh in range(2):
                w = {"ve": prev_cast["ve"]} if prev_cast else {}
                if n == "wo" and j == 0 and hh == 0:
                    w["dma"] = consts_done["dma"]
                ld = op("dw", (lambda e, n=n, j=j, hh=hh: e.dma_start(
                    out=S["stage"][:],
                    in_=W_dram[n][128 * j:128 * j + 128, 512 * hh:512 * hh + 512])),
                    waits=w, inc=16)
                co = 1024 * j + 512 * hh
                fn = (lambda e, n=n, co=co: e.tensor_copy(
                    out=S[n + "s"][:, co:co + 512], in_=S["stage"][:]))
                prev_cast = op("ve", fn, waits={"dw": ld["dw"]})
    for j in range(8):
        ld = op("dw", (lambda e, j=j: e.dma_start(
            out=S["stage"][:, 0:136], in_=flo[128 * j:128 * j + 128, :])),
            waits={"ve": prev_cast["ve"]}, inc=16)
        prev_cast = op("ve", (lambda e, j=j: e.tensor_copy(
            out=S["flos"][:, 136 * j:136 * j + 136], in_=S["stage"][:, 0:136])),
            waits={"dw": ld["dw"]})
    for j in range(8):
        ld = op("dw", (lambda e, j=j: e.dma_start(
            out=S["stage"][:, 0:48], in_=wba[128 * j:128 * j + 128, :])),
            waits={"ve": prev_cast["ve"]}, inc=16)

        op("ve", (lambda e, j=j: e.tensor_copy(
            out=S["wbas"][:, 48 * j:48 * j + 32], in_=S["stage"][:, 0:32])),
            waits={"dw": ld["dw"]})
        prev_cast = op("ve", (lambda e, j=j: e.tensor_scalar(
            out=S["wbas"][:, 48 * j + 32:48 * j + 48],
            in0=S["stage"][:, 32:48], scalar1=-1.0, scalar2=None, op0=ALU.mult)))
    for hh in range(2):
        ld = op("dw", (lambda e, hh=hh: e.dma_start(
            out=S["stage"][:], in_=fv2[:, 512 * hh:512 * hh + 512])),
            waits={"ve": prev_cast["ve"]}, inc=16)
        prev_cast = op("ve", (lambda e, hh=hh: e.tensor_copy(
            out=S["fv2s"][:, 512 * hh:512 * hh + 512], in_=S["stage"][:])),
            waits={"dw": ld["dw"]})

    op("ve", lambda e: e.memset(S["bias48"][:], 0.0),
       waits={"dma": consts_done["dma"]})
    op("ve", lambda e: e.tensor_tensor(out=S["bias48"][0:16, :],
                                       in0=S["smalls_s"][:, 0:1],
                                       in1=S["smalls_s"][:, 1:2], op=ALU.add))
    bias_done = op("ve", lambda e: e.tensor_scalar(
        out=S["bias48"][32:48, :], in0=S["smalls_s"][:, 2:3],
        scalar1=-1.0, scalar2=None, op0=ALU.mult))
    ea = op("ac", lambda e: e.activation(out=S["negea"][:], in_=S["smalls_s"][:, 3:4],
                                         func=AF.Exp),
            waits={"dma": consts_done["dma"], "ve": 1})
    ea2 = op("ve", lambda e: e.tensor_scalar(out=S["negea"][:], in0=S["negea"][:],
                                             scalar1=-1.0, scalar2=None, op0=ALU.mult),
             waits={"ac": ea["ac"]})
    dpc = op("ve", lambda e: e.tensor_copy(out=S["dpb"][:], in_=S["smalls_s"][:, 4:5]),
             waits={"dma": consts_done["dma"]})
    m1 = op("te", lambda e: e.matmul(out=psS[0:1, 0:16], lhsT=S["dpb"][:],
                                     rhs=S["idbf_s"][0:16, 0:16], start=True, stop=True,
                                     skip_group_check=True),
            waits={"ve": dpc["ve"]})
    e1 = op("ve", lambda e: e.tensor_copy(out=S["dprowb"][:], in_=psS[0:1, 0:16]),
            waits={"te": m1["te"]})
    m2 = op("te", lambda e: e.matmul(out=psS[:, 16:32], lhsT=S["ones3_s"][0:1, :],
                                     rhs=S["dprowb"][:], start=True, stop=True,
                                     skip_group_check=True),
            waits={"ve": e1["ve"]})
    bo4dp_d = op("ve", lambda e: e.tensor_tensor(
        out=S["bo4dp"][:].rearrange("p (i h) -> p i h", i=4),
        in0=S["bo4f_s"][:].rearrange("p (i h) -> p i h", i=4),
        in1=psS[:, 16:32].unsqueeze(1).broadcast_to([128, 4, 16]),
        op=ALU.mult), waits={"te": m2["te"]})

    def f_zero(e):
        for nm in ["SbE", "SbO", "Sb", "Gfm", "splg", "splp", "tails", "cvb"]:
            e.memset(S[nm][:], 0.0)
        return e.memset(S["Sf"][:], 0.0)
    zero_d = op("ve", f_zero, waits={"te": m2["te"]})

    x0 = op("dx", lambda e: e.dma_start(out=S["xin"][:, 0:1024], in_=x_in[0:128, :]),
            inc=16)
    op("dcs", lambda e: e.dma_start(out=S["csb"][:, 0:128], in_=cs[:, 0:128]), inc=16)
    cs0 = op("dcs", lambda e: e.dma_start(out=S["csb"][:, 128:256], in_=cs[:, T:T + 128]),
             inc=16)

    prolog = dict(cnt)
    last_xdma = x0
    last_cs = cs0

    def TS(ek, out, in0, s1, s2=None, op0=ALU.mult, op1=None, waits=None):
        if op1 is None:
            return op(ek, (lambda e: e.tensor_scalar(out=out, in0=in0, scalar1=s1,
                                                     scalar2=s2, op0=op0)), waits=waits)
        return op(ek, (lambda e: e.tensor_scalar(out=out, in0=in0, scalar1=s1, scalar2=s2,
                                                 op0=op0, op1=op1)), waits=waits)

    def TT(ek, out, in0, in1, alu=ALU.mult, waits=None):
        return op(ek, (lambda e: e.tensor_tensor(out=out, in0=in0, in1=in1, op=alu)),
                  waits=waits)

    def MM(out, lhsT, rhs, start, stop, waits=None, tpos=None):
        return op("te", (lambda e: e.matmul(out=out, lhsT=lhsT, rhs=rhs, start=start,
                                            stop=stop, skip_group_check=True,
                                            tile_position=tpos)), waits=waits)

    def TR(out, in_, ident, waits=None):
        return op("te", (lambda e: e.transpose(out=out, in_=in_, identity=ident)),
                  waits=waits)

    def ACT(out, in_, func, bias=0.0, scale=1.0, accum=None, waits=None):
        return op("ac", (lambda e: e.activation(out=out, in_=in_, func=func, bias=bias,
                                                scale=scale, accum_out=accum)),
                  waits=waits)

    def CP(ek, out, in_, waits=None):
        return op(ek, (lambda e: e.tensor_copy(out=out, in_=in_)), waits=waits)

    wq_s, wk_s, wv_s, wg_s = S["wqs"], S["wks"], S["wvs"], S["wgs"]

    for c in range(nsc):
        xin_ap = S["xin"][:]

        def ln(src, wfirst):
            rw = S["rw"]
            a = op("ve", (lambda e: e.reduce_sum(out=rw[:, 0:1], in_=src, axis=AXX)),
                   waits=wfirst)
            TS("ve", rw[:, 0:1], rw[:, 0:1], 1.0 / HID)
            b = ACT(S["ab"][:], src, AF.Square, accum=rw[:, 2:3], waits=wfirst)
            TT("ve", rw[:, 1:2], rw[:, 0:1], rw[:, 0:1], ALU.mult, waits={"ac": b["ac"]})
            op("ve", lambda e: e.scalar_tensor_tensor(
                out=rw[:, 3:4], in0=rw[:, 2:3], scalar=1.0 / HID, in1=rw[:, 1:2],
                op0=ALU.mult, op1=ALU.subtract))
            s1 = ACT(rw[:, 4:5], rw[:, 3:4], AF.Ln, bias=1e-5, waits={"ve": cnt["ve"]})
            s1 = ACT(rw[:, 3:4], rw[:, 4:5], AF.Exp, scale=-0.5)
            lnm = TS("ve", S["lnt"][:], src, rw[:, 0:1], rw[:, 3:4],
                     ALU.subtract, ALU.mult, waits={"ac": s1["ac"]})
            evs = None
            for g in range(2):
                tp = None
                for j in range(4):
                    jj = 4 * g + j
                    w = {"ve": lnm["ve"]}
                    if evs:
                        w["ac"] = evs["ac"]
                    tp = TR(psT[:, 128 * j:128 * j + 128],
                            S["lnt"][:, 128 * jj:128 * jj + 128], S["idbf_s"][:], waits=w)
                evs = op("ac", (lambda e, g=g: e.copy(
                    out=S["lnT"][:, 512 * g:512 * g + 512], in_=psT[:])),
                    waits={"te": tp["te"]})
            return evs

        def ffn(x_src, out_ap, wln):
            mmb = None
            for j in range(8):
                for hh in range(2):
                    mmb = MM(psB[:, 512 * hh:512 * hh + 512],
                             S["lnT"][:, 128 * j:128 * j + 128],
                             S["fws"][:, 1024 * j + 512 * hh:1024 * j + 512 * hh + 512],
                             j == 0, False,
                             waits={"ac": wln["ac"]} if j == 0 and hh == 0 else None)
            for j in range(8):
                mmb = MM(psA[:, 0:136], S["lnT"][:, 128 * j:128 * j + 128],
                         S["flos"][:, 136 * j:136 * j + 136], j == 0, j == 7)
            rw = S["rw"]
            a = op("ve", (lambda e: e.reduce_max(out=rw[:, 5:6], in_=psA[:, 128:136],
                                                 axis=AXX)), waits={"te": mmb["te"]})
            TS("ve", rw[:, 5:6], rw[:, 5:6], -1.0)
            b = op("ac", (lambda e: e.activation(out=S["gates"][:], in_=psA[:, 128:136],
                                                 func=AF.Exp, bias=rw[:, 5:6])),
                   waits={"ve": cnt["ve"]})
            op("ve", (lambda e: e.reduce_sum(out=rw[:, 5:6], in_=S["gates"][:], axis=AXX)),
               waits={"ac": b["ac"]})
            op("ve", lambda e: e.reciprocal(out=rw[:, 5:6], in_=rw[:, 5:6]))
            TS("ve", rw[:, 5:6], rw[:, 5:6], 1.0 / 16.0)
            TS("ve", S["gates"][:], S["gates"][:], rw[:, 5:6])
            tbm = op("ve", lambda e: e.tensor_tensor(
                out=S["tb"][:].rearrange("p (g r) -> p g r", g=8),
                in0=psA[:, 0:128].rearrange("p (g r) -> p g r", g=8),
                in1=S["gates"][:].unsqueeze(-1).broadcast_to([128, 8, 16]),
                op=ALU.mult))
            tp = TR(psT[:, 0:128], S["tb"][:], S["idbf_s"][:],
                    waits={"ve": tbm["ve"], "ac": wln["ac"]})
            ev = op("ac", lambda e: e.copy(out=S["r0"][:], in_=psT[:, 0:128]),
                    waits={"te": tp["te"]})
            lm = MM(psB[:, 0:512], S["r0"][:], S["fv2s"][:, 0:512], False, True,
                    waits={"ac": ev["ac"]})
            lm = MM(psB[:, 512:1024], S["r0"][:], S["fv2s"][:, 512:1024], False, True)
            return op("ve", (lambda e: e.scalar_tensor_tensor(
                out=out_ap, in0=psB[:], scalar=0.5, in1=x_src,
                op0=ALU.mult, op1=ALU.add)), waits={"te": lm["te"]})

        w0 = {"dx": last_xdma["dx"]}
        if c > 0:
            w0["dy"] = last_od["dy"]
        if c == 0:
            w0["ve"] = max(prev_cast["ve"], bias_done["ve"], ea2["ve"], bo4dp_d["ve"],
                           zero_d["ve"])
        else:
            w0["ve"] = sc_end_ve
            w0["te"] = sc_end_te
        ln1 = ln(xin_ap, w0)
        x1d = ffn(xin_ap, S["xt"][:], ln1)

        if phase <= 1:
            sc_end_ve = cnt["ve"]; sc_end_te = cnt["te"]
            continue
        if c + 1 < nsc:
            last_xdma = op("dx", (lambda e, c=c: e.dma_start(
                out=S["xin"][:],
                in_=x_in[128 * (c + 1):128 * (c + 1) + 128, :])),
                waits={"ve": x1d["ve"]}, inc=16)

        ln2 = ln(S["xt"][:], {"ve": x1d["ve"]})

        mm = None
        for j in range(8):
            mm = MM(psS[0:48, 0:128], S["wbas"][:, 48 * j:48 * j + 48],
                    S["lnT"][:, 128 * j:128 * j + 128], j == 0, j == 7,
                    waits={"ac": ln2["ac"]} if j == 0 else None)
        zw = {"te": mm["te"]}
        ACT(S["spall"][:], psS[0:48, 0:128], AF.Abs, bias=S["bias48"][:], waits=zw)
        ACT(S["ab"][0:48, 0:128], S["spall"][:], AF.Exp, scale=-1.0)
        a3 = ACT(S["spall"][:], S["ab"][0:48, 0:128], AF.Ln, bias=1.0)
        a4 = ACT(S["ab"][0:48, 0:128], psS[0:48, 0:128], AF.Relu, bias=S["bias48"][:],
                 waits=zw)
        a5 = TT("ve", S["sp2"][:], S["spall"][:], S["ab"][0:48, 0:128], ALU.add,
                waits={"ac": a4["ac"]})
        TS("ve", S["lnbfm"][:], S["sp2"][32:48, :], -1.0, waits={"ve": a5["ve"]})
        bsg = ACT(S["bfm"][:], S["sp2"][32:48, :], AF.Exp, scale=-1.0,
                  waits={"ve": a5["ve"]})
        glg = TS("ve", S["glogfm"][:], S["sp2"][0:16, :], S["negea"][:, 0:1])
        gsc = op("ve", lambda e: e.tensor_tensor_scan(
            out=S["Gfm"][:, 1:129], data0=S["glogfm"][:], data1=S["glogfm"][:],
            initial=0.0, op0=ALU.add, op1=ALU.bypass), waits={"ve": glg["ve"]})
        gpl = TT("ve", S["gplfm"][:], S["Gfm"][:, 1:129], S["lnbfm"][:], ALU.add)

        def split3(src, dst):
            TS("ve", S["rsq"][:], src, 2.0 ** 29, 2.0 ** 29, ALU.add, ALU.subtract)
            TS("ve", S["t025"][:], src, 2.0 ** 21, 2.0 ** 21, ALU.add, ALU.subtract)
            CP("ve", dst[:, 1:129], S["rsq"][:])
            TT("ve", dst[:, 130:258], S["t025"][:], S["rsq"][:], ALU.subtract)
            return TT("ve", dst[:, 259:387], src, S["t025"][:], ALU.subtract)
        spg = split3(S["Gfm"][:, 1:129], S["splg"])
        spp = split3(S["gplfm"][:], S["splp"])
        fd = None
        for j in range(3):
            fd = op("gp", (lambda e, j=j: e.dma_start(
                out=S["repA"][j:j + 1, :],
                in_=S["splp"][:, 129 * j + 1:129 * j + 129])),
                waits={"ve": spp["ve"]}, inc=16)
            fd = op("gp", (lambda e, j=j: e.dma_start(
                out=S["repM"][j:j + 1, :],
                in_=S["splg"][:, 129 * j + 0:129 * j + 128])),
                waits={"ve": spg["ve"]}, inc=16)
        rep_dma = fd

        tws = {"ve": gpl["ve"], "ac": bsg["ac"]}
        TR(psS[:, 128:144], S["Gfm"][:, 1:129], S["idf32_s"][0:16, 0:16], waits=tws)
        TR(psS[:, 144:160], S["Gfm"][:, 0:128], S["idf32_s"][0:16, 0:16])
        TR(psS[:, 160:176], S["gplfm"][:], S["idf32_s"][0:16, 0:16])
        t4 = TR(psS[:, 176:192], S["bfm"][:], S["idf32_s"][0:16, 0:16])
        gtm_ev = CP("ve", S["gtm"][:], psS[:, 128:144], waits={"te": t4["te"]})
        egp = ACT(S["egprev"][:], psS[:, 144:160], AF.Exp, waits={"te": t4["te"]})
        bge = ACT(S["bgetm"][:], psS[:, 160:176], AF.Exp, waits={"te": t4["te"]})
        btm_ev = CP("ve", S["btm"][:], psS[:, 176:192], waits={"te": t4["te"]})
        g1 = op("dg", lambda e: e.dma_start(out=S["gcrow"][:], in_=S["Gfm"][:, 128:129]),
                waits={"ve": gsc["ve"]}, inc=16)
        g2a = TS("ve", S["gc64"][:], S["gcrow"][:], 2.0 ** 29, 2.0 ** 29, ALU.add,
                 ALU.subtract, waits={"dg": g1["dg"]})
        TS("ve", S["gc025"][:], S["gcrow"][:], 2.0 ** 21, 2.0 ** 21, ALU.add, ALU.subtract)
        CP("ve", S["gA"][:], S["gc64"][:])
        TT("ve", S["gB"][:], S["gc025"][:], S["gc64"][:], ALU.subtract)
        g2b = TT("ve", S["gC"][:], S["gcrow"][:], S["gc025"][:], ALU.subtract)
        op("dg", lambda e: e.dma_start(out=S["grow3"][0:1, :], in_=S["gA"][:]),
           waits={"ve": g2b["ve"]}, inc=16)
        op("dg", lambda e: e.dma_start(out=S["grow3"][1:2, :], in_=S["gB"][:]), inc=16)
        g2c = op("dg", lambda e: e.dma_start(out=S["grow3"][2:3, :], in_=S["gC"][:]),
                 inc=16)
        g3 = MM(psS[:, 208:224], S["ones3_s"][:], S["grow3"][:], True, True,
                waits={"dg": g2c["dg"]})
        ad = TT("ve", S["ab"][:, 0:16], psS[:, 208:224], S["gtm"][:], ALU.subtract,
                waits={"te": g3["te"], "ve": gtm_ev["ve"]})
        exd = ACT(S["expdel"][:], S["ab"][:, 0:16], AF.Exp, waits={"ve": ad["ve"]})
        gm = ACT(S["gam"][:], psS[0:64, 208:224], AF.Exp, waits={"te": g3["te"]})

        if phase <= 2:
            sc_end_ve = cnt["ve"]; sc_end_te = cnt["te"]
            continue
        prev_ev = {"ve": cnt["ve"]}
        sil_d = {}
        ab_guard = exd
        for wn, pn, pi in [("wq", "q", 0), ("wk", "k", 1), ("wv", "v", 2), ("wg", "g", 3)]:
            mm = None
            for ot in range(8):
                for j in range(8):
                    w = None
                    if ot == 0 and j == 0:
                        w = {"ac": ln2["ac"], "ve": prev_ev["ve"]}
                    mm = MM(psB[:, 128 * ot:128 * ot + 128],
                            S[wn + "s"][:, 1024 * j + 128 * ot:1024 * j + 128 * ot + 128],
                            S["lnT"][:, 128 * j:128 * j + 128], j == 0, j == 7, waits=w)
            if pn == "g":
                prev_ev = CP("ve", S["gsil"][:], psB[:], waits={"te": mm["te"]})
                continue
            cvr = S["cvb"][:].rearrange("p (b w) -> p b w", w=131)
            t_new = op("ve", (lambda e, pi=pi: e.tensor_copy(
                out=S["ab"][:, 0:24].rearrange("p (b w) -> p b w", w=3),
                in_=psB[:].rearrange("p (b w) -> p b w", w=128)[:, :, 125:128])),
                waits={"te": mm["te"], "ac": ab_guard["ac"]})
            ev1 = op("ve", (lambda e: e.tensor_copy(
                out=S["cvb"][:].rearrange("p (b w) -> p b w", w=131)[:, :, 3:131],
                in_=psB[:].rearrange("p (b w) -> p b w", w=128))))
            ev2 = op("ve", (lambda e, pi=pi: e.tensor_copy(
                out=S["cvb"][:].rearrange("p (b w) -> p b w", w=131)[:, :, 0:3],
                in_=S["tails"][:, 24 * pi:24 * pi + 24]
                .rearrange("p (b w) -> p b w", w=3))))
            ev3 = op("ve", (lambda e, pi=pi: e.tensor_copy(
                out=S["tails"][:, 24 * pi:24 * pi + 24], in_=S["ab"][:, 0:24])))
            dst = {"q": "qsil", "k": "ksil", "v": "vsil"}[pn]
            st = None
            for bkt in range(8):
                base = 131 * bkt
                cwc = S["cws"][:, 12 * bkt + 4 * pi:12 * bkt + 4 * pi + 4]
                st = TS("ve", S["ab"][:, 128 * bkt:128 * bkt + 128],
                        S["cvb"][:, base + 0:base + 128], cwc[:, 0:1],
                        waits={"ve": ev3["ve"]} if bkt == 0 else None)
                for tap in range(1, 4):
                    st = op("ve", (lambda e, bkt=bkt, tap=tap, base=base, cwc=cwc:
                                   e.scalar_tensor_tensor(
                        out=S["ab"][:, 128 * bkt:128 * bkt + 128],
                        in0=S["cvb"][:, base + tap:base + tap + 128],
                        scalar=cwc[:, tap:tap + 1],
                        in1=S["ab"][:, 128 * bkt:128 * bkt + 128],
                        op0=ALU.mult, op1=ALU.add)))
            sl = None
            for bkt in range(8):
                sl = op("ac", (lambda e, bkt=bkt, pi=pi, dst=dst: e.activation(
                    out=S[dst][:, 128 * bkt:128 * bkt + 128],
                    in_=S["ab"][:, 128 * bkt:128 * bkt + 128], func=AF.Silu,
                    bias=S["cbs"][:, 3 * bkt + pi:3 * bkt + pi + 1])),
                    waits={"ve": st["ve"]} if bkt == 0 else None)
            sil_d[pn] = sl
            ab_guard = sl
            prev_ev = ev1

        if phase <= 3:
            sc_end_ve = cnt["ve"]; sc_end_te = cnt["te"]
            continue
        csl = S["csb"][:, 0:128]
        ssl = S["csb"][:, 128:256]
        csw = {"dcs": last_cs["dcs"]}
        prev_mm_guard = None
        for srcn, dstn in [("qsil", "qsil"), ("ksil", "ksil")]:
            w1 = {"ac": sil_d["k" if srcn == "ksil" else "q"]["ac"]}
            w1.update(csw)
            if prev_mm_guard is not None:
                w1["te"] = prev_mm_guard
            csl4 = csl.unsqueeze(1).broadcast_to([128, 4, 128])
            ssl4 = ssl.unsqueeze(1).broadcast_to([128, 4, 128])
            def v3(ap):
                return ap.rearrange("p (a c) -> p a c", a=4)
            TT("ve", v3(S["X0"][:, 1024:1536]), v3(S[srcn][:, 0:512]), csl4,
               ALU.mult, waits=w1)
            TT("ve", v3(S["X0"][:, 1536:2048]), v3(S[srcn][:, 0:512]), ssl4, ALU.mult)
            TT("ve", v3(S["KKm"][:, 0:512]), v3(S[srcn][:, 512:1024]), ssl4, ALU.mult)
            TT("ve", v3(S["KKm"][:, 512:1024]), v3(S[srcn][:, 512:1024]), csl4, ALU.mult)
            TT("ve", S["X1"][:, 0:512], S["X0"][:, 1024:1536], S["KKm"][:, 0:512],
               ALU.subtract)
            v6 = TT("ve", S["X1"][:, 512:1024], S["X0"][:, 1536:2048],
                    S["KKm"][:, 512:1024], ALU.add)
            sq = ACT(S["X0"][:, 0:512], S["X1"][:, 0:512], AF.Square,
                     waits={"ve": v6["ve"]})
            sq = ACT(S["X0"][:, 512:1024], S["X1"][:, 512:1024], AF.Square)
            sqmm = None
            for j in range(8):
                sqmm = MM(psS[0:16, 224:352], S["bo4b_s"][:, 16 * (j % 4):16 * (j % 4) + 16],
                          S["X0"][:, 128 * j:128 * j + 128], j == 0, j == 7,
                          waits={"ac": sq["ac"]} if j == 0 else None)
            rq2 = ACT(S["sp2"][0:16, :], psS[0:16, 224:352], AF.Ln, bias=1e-12,
                      waits={"te": sqmm["te"]})
            rq3 = ACT(S["rsq"][:], S["sp2"][0:16, :], AF.Exp, scale=-0.5)
            rq4 = CP("ve", S["rsqb"][:], S["rsq"][:], waits={"ac": rq3["ac"]})
            rp = None
            for pr in range(4):
                rp = MM(psA[:, 128 * pr:128 * pr + 128],
                        S["dn4_s"][:, 128 * pr:128 * pr + 128],
                        S["rsqb"][:], True, True,
                        waits={"ve": rq4["ve"]} if pr == 0 else None)
            rc = op("ac", lambda e: e.copy(out=S["E1"][:, 0:512], in_=psA[:, 0:512]),
                    waits={"te": rp["te"]})
            TT("ve", S[dstn][:, 0:512], S["X1"][:, 0:512], S["E1"][:, 0:512],
               ALU.mult, waits={"ac": rc["ac"]})
            lsq = TT("ve", S[dstn][:, 512:1024], S["X1"][:, 512:1024],
                     S["E1"][:, 0:512], ALU.mult)
            prev_mm_guard = rp["te"]
        ql2 = lsq

        if c + 1 < nsc:
            last_cs = op("dcs", (lambda e, c=c: e.dma_start(
                out=S["csb"][:, 0:128],
                in_=cs[:, 128 * (c + 1):128 * (c + 1) + 128])),
                waits={"ve": ql2["ve"]}, inc=16)
            last_cs = op("dcs", (lambda e, c=c: e.dma_start(
                out=S["csb"][:, 128:256],
                in_=cs[:, T + 128 * (c + 1):T + 128 * (c + 1) + 128])), inc=16)

        if phase <= 4:
            sc_end_ve = cnt["ve"]; sc_end_te = cnt["te"]
            continue
        evk = None
        for g in range(4):
            src = S["ksil"] if g < 2 else S["vsil"]
            dstn = "ktm" if g < 2 else "vtm"
            gg = g % 2
            tp = None
            for j in range(4):
                jj = 4 * gg + j
                w = {"ve": ql2["ve"]} if g == 0 and j == 0 else None
                if evk and j == 0:
                    w = {"ac": evk["ac"]}
                tp = TR(psT[:, 128 * j:128 * j + 128], src[:, 128 * jj:128 * jj + 128],
                        S["idbf_s"][:], waits=w)
            evk = op("ac", (lambda e, dstn=dstn, gg=gg: e.copy(
                out=S[dstn][:, 512 * gg:512 * gg + 512], in_=psT[:])),
                waits={"te": tp["te"]})

        if phase <= 4.2:
            sc_end_ve = cnt["ve"]; sc_end_te = cnt["te"]
            continue
        kd = None
        for (srcn, dstn, rr) in [("ksil", "X0", 0), ("qsil", "X1", 0)]:
            for a in range(4):
                for eo in range(2):
                    w = {"ve": ql2["ve"]} if kd is None else None
                    kd = op("dk", (lambda e, srcn=srcn, dstn=dstn, a=a, eo=eo: e.dma_start(
                        out=S[dstn][32 * eo:32 * eo + 32, :]
                        .rearrange("p (m c) -> p m c", m=16)[:, a::4, :],
                        in_=S[srcn][32 * a:32 * a + 32, 512 * eo:512 * eo + 512]
                        .rearrange("p (b c) -> p b c", b=4))), waits=w, inc=16)
        krs_dma = kd

        mmr = None
        for seg in range(4):
            mmr = MM(psA[:, 512 * seg:512 * seg + 512], S["ones3_s"][:],
                     S["repA"][:, 512 * seg:512 * seg + 512], True, True,
                     waits={"gp": rep_dma["gp"], "ac": evk["ac"]} if seg == 0 else None)
        gtb = S["gtm"][:].unsqueeze(-1).broadcast_to([128, 16, 128])
        op("ve", (lambda e: e.tensor_tensor(
            out=S["arg1"][:].rearrange("p (h t) -> p h t", h=16),
            in0=psA[:].rearrange("p (h t) -> p h t", h=16),
            in1=gtb, op=ALU.subtract)),
            waits={"te": mmr["te"], "ve": gtm_ev["ve"]})
        ats = TS("ve", S["arg1"][:], S["arg1"][:], 0.0, None, ALU.min)
        eA = ACT(S["E1"][:], S["arg1"][:], AF.Exp, waits={"ve": ats["ve"]})
        if phase <= 4.4:
            sc_end_ve = cnt["ve"]; sc_end_te = cnt["te"]
            continue
        mmk = None
        for h in range(H):
            kS = S["X0"][0:64, 128 * h:128 * h + 128]
            w = {"ve": ats["ve"], "dk": krs_dma["dk"]} if h == 0 else None
            mmk = MM(psA[:, 128 * h:128 * h + 128], kS, kS, True, True, waits=w)
        if phase <= 4.5:
            sc_end_ve = cnt["ve"]; sc_end_te = cnt["te"]
            continue
        kkev = op("ve", lambda e: e.tensor_tensor(
            out=S["KKm"][:].rearrange("p (h t) -> p h t", h=16),
            in0=psA[:].rearrange("p (h t) -> p h t", h=16),
            in1=S["mneg"][:].unsqueeze(1).broadcast_to([128, 16, 128]),
            op=ALU.mult), waits={"te": mmk["te"]})
        nt0 = TT("ve", S["NT0"][:], S["E1"][:], S["KKm"][:], ALU.mult,
                 waits={"ac": eA["ac"]})
        if phase <= 4.6:
            sc_end_ve = cnt["ve"]; sc_end_te = cnt["te"]
            continue
        mmr = None
        for seg in range(4):
            mmr = MM(psA[:, 512 * seg:512 * seg + 512], S["ones3_s"][:],
                     S["repM"][:, 512 * seg:512 * seg + 512], True, True,
                     waits={"ve": kkev["ve"]} if seg == 0 else None)
        op("ve", (lambda e: e.tensor_tensor(
            out=S["arg1"][:].rearrange("p (h t) -> p h t", h=16),
            in0=psA[:].rearrange("p (h t) -> p h t", h=16),
            in1=gtb, op=ALU.subtract)), waits={"te": mmr["te"]})
        ats = TS("ve", S["arg1"][:], S["arg1"][:], 0.0, None, ALU.min)
        eM = ACT(S["E1"][:], S["arg1"][:], AF.Exp, waits={"ve": ats["ve"]})
        mmk = None
        for h in range(H):
            kS = S["X0"][0:64, 128 * h:128 * h + 128]
            qS = S["X1"][0:64, 128 * h:128 * h + 128]
            w = {"ve": ats["ve"]} if h == 0 else None
            mmk = MM(psA[:, 128 * h:128 * h + 128], kS, qS, True, True, waits=w)
        if phase <= 4.8:
            sc_end_ve = cnt["ve"]; sc_end_te = cnt["te"]
            continue
        kqev = op("ve", lambda e: e.tensor_tensor(
            out=S["KKm"][:].rearrange("p (h t) -> p h t", h=16),
            in0=psA[:].rearrange("p (h t) -> p h t", h=16),
            in1=S["mpos"][:].unsqueeze(1).broadcast_to([128, 16, 128]),
            op=ALU.mult), waits={"te": mmk["te"]})
        mtd = TT("ve", S["MT"][:], S["E1"][:], S["KKm"][:], ALU.mult,
                 waits={"ac": eM["ac"]})

        if phase <= 5:
            sc_end_ve = cnt["ve"]; sc_end_te = cnt["te"]
            continue
        xb1 = op("ve", lambda e: e.tensor_tensor(
            out=S["X0"][:].rearrange("p (h z) -> p h z", h=16)[:, :, 0:64],
            in0=S["vtm"][:].rearrange("p (h d) -> p h d", h=16),
            in1=S["btm"][:].unsqueeze(-1).broadcast_to([128, 16, 64]),
            op=ALU.mult), waits={"ve": mtd["ve"], "ac": evk["ac"]})
        xb2 = op("ve", lambda e: e.tensor_tensor(
            out=S["X0"][:].rearrange("p (b a z) -> p b a z", b=4, a=4)[:, :, :, 64:128]
            .rearrange("p b a (eo d) -> p b a eo d", eo=2),
            in0=S["ktm"][:].rearrange("p (eo b a d) -> p b a eo d", eo=2, b=4, a=4),
            in1=S["bgetm"][:].rearrange("p (b a) -> p b a", b=4)
            .unsqueeze(-1).unsqueeze(-1).broadcast_to([128, 4, 4, 2, 32]),
            op=ALU.mult))
        kp1 = op("ve", lambda e: e.tensor_tensor(
            out=S["Kp"][:].rearrange("p (b a z) -> p b a z", b=4, a=4)
            .rearrange("p b a (eo d) -> p b a eo d", eo=2),
            in0=S["ktm"][:].rearrange("p (eo b a d) -> p b a eo d", eo=2, b=4, a=4),
            in1=S["expdel"][:].rearrange("p (b a) -> p b a", b=4)
            .unsqueeze(-1).unsqueeze(-1).broadcast_to([128, 4, 4, 2, 32]),
            op=ALU.mult), waits={"ac": exd["ac"]})

        evn = None
        for g in range(4):
            tp = None
            for j in range(4):
                jj = 4 * g + j
                w = {"ve": nt0["ve"]} if g == 0 and j == 0 else None
                if evn and j == 0:
                    w = {"ac": evn["ac"]}
                tp = TR(psT[:, 128 * j:128 * j + 128],
                        S["NT0"][:, 128 * jj:128 * jj + 128], S["idbf_s"][:], waits=w)
            evn = op("ac", (lambda e, g=g: e.copy(
                out=S["NU0"][:, 512 * g:512 * g + 512], in_=psT[:])),
                waits={"te": tp["te"]})

        NTc, NTn = "NT0", "NT1"
        Xc, Xn = "X0", "X1"
        last_ac = evn
        last_ve = {"ve": max(xb1["ve"], xb2["ve"], kp1["ve"])}
        def halves(src_a, src_b_name, dst_fn, evac_eng, wait0, alu_or_copy):
            mms = [None, None]
            evs = [None, None]
            for half in range(2):
                for i in range(8):
                    h = 8 * half + i
                    w = None
                    if half == 0 and i == 0:
                        w = dict(wait0)
                    if half == 1 and i == 0 and evs[0] is None:
                        w = None
                    mms[half] = MM(psA[:, 128 * h:128 * h + 128],
                                   src_a(h), S[src_b_name][:, 128 * h:128 * h + 128],
                                   True, True, waits=w)
                evs[half] = dst_fn(half, mms[half])
            return evs[1]
        for lvl in range(7):
            w0l = {"ac": last_ac["ac"], "ve": last_ve["ve"]}
            mmA = [None, None]
            for half in range(2):
                for i in range(8):
                    h = 8 * half + i
                    w = dict(w0l) if (half == 0 and i == 0) else None
                    if half == 1 and i == 0:
                        w = None
                    mmA[half] = MM(psA[:, 128 * h:128 * h + 128],
                                   S[NTc][:, 128 * h:128 * h + 128],
                                   S[Xc][:, 128 * h:128 * h + 128], True, True, waits=w)
            x1e = TT("ve", S[Xn][:, 0:1024], psA[:, 0:1024], S[Xc][:, 0:1024], ALU.add,
                     waits={"te": mmA[0]["te"]})
            xev = TT("ve", S[Xn][:, 1024:2048], psA[:, 1024:2048], S[Xc][:, 1024:2048],
                     ALU.add, waits={"te": mmA[1]["te"]})
            last_ve = xev
            if lvl < 6:
                mmB = [None, None]
                for half in range(2):
                    for i in range(8):
                        h = 8 * half + i
                        w = None
                        if half == 0 and i == 0:
                            w = {"ve": x1e["ve"]}
                        if half == 1 and i == 0:
                            w = {"ve": xev["ve"]}
                        mmB[half] = MM(psA[:, 128 * h:128 * h + 128],
                                       S["NU0"][:, 128 * h:128 * h + 128],
                                       S[NTc][:, 128 * h:128 * h + 128], True, True,
                                       waits=w)
                n1 = op("ac", (lambda e, NTn=NTn: e.copy(out=S[NTn][:, 0:1024],
                                                         in_=psA[:, 0:1024])),
                        waits={"te": mmB[0]["te"]})
                ntc = op("ac", (lambda e, NTn=NTn: e.copy(out=S[NTn][:, 1024:2048],
                                                          in_=psA[:, 1024:2048])),
                        waits={"te": mmB[1]["te"]})
                last_ac = ntc
                if lvl < 5:
                    mmC = [None, None]
                    for half in range(2):
                        for i in range(8):
                            h = 8 * half + i
                            w = None
                            if half == 0 and i == 0:
                                w = {"ac": n1["ac"]}
                            if half == 1 and i == 0:
                                w = {"ac": ntc["ac"]}
                            mmC[half] = MM(psA[:, 128 * h:128 * h + 128],
                                           S[NTc][:, 128 * h:128 * h + 128],
                                           S["NU0"][:, 128 * h:128 * h + 128], True,
                                           True, waits=w)
                    u1 = op("ac", lambda e: e.copy(out=S["NU0"][:, 0:1024],
                                                   in_=psA[:, 0:1024]),
                            waits={"te": mmC[0]["te"]})
                    evn2 = op("ac", lambda e: e.copy(out=S["NU0"][:, 1024:2048],
                                                     in_=psA[:, 1024:2048]),
                             waits={"te": mmC[1]["te"]})
                    last_ac = evn2
            NTc, NTn = NTn, NTc
            Xc, Xn = Xn, Xc
        Xf = Xc

        if phase <= 6:
            sc_end_ve = cnt["ve"]; sc_end_te = cnt["te"]
            continue
        mmw = None
        for h in range(H):
            w = {"ve": last_ve["ve"]} if h == 0 else None
            mmw = MM(psA[0:64, 128 * h:128 * h + 128],
                     S[Xf][:, 128 * h + 64:128 * h + 128],
                     S["MT"][:, 128 * h:128 * h + 128], True, True, waits=w)
        nmk = TS("ve", S["arg1"][0:64, :], psA[0:64, :], -1.0, waits={"te": mmw["te"]})
        mmw = None
        for h in range(H):
            w = {"ve": nmk["ve"]} if h == 0 else None
            mmw = MM(psB[0:64, 64 * h:64 * h + 64],
                     S[Xf][:, 128 * h + 64:128 * h + 128],
                     S["Kp"][:, 64 * h:64 * h + 64], True, True, waits=w)
        nuk = TS("ve", S["E1"][0:64, 0:1024], psB[0:64, :], -1.0, waits={"te": mmw["te"]})

        mq = None
        for h in range(H):
            a, b2 = h % 4, h // 4
            w = {"ve": nuk["ve"]} if h == 0 else None
            mq = MM(psA[:, 128 * h:128 * h + 64], S["qsil"][:, 128 * b2:128 * b2 + 128],
                    S["SbE"][:, 64 * h:64 * h + 64], True, False, waits=w)
            mq = MM(psA[:, 128 * h:128 * h + 64],
                    S["qsil"][:, 512 + 128 * b2:512 + 128 * b2 + 128],
                    S["SbO"][:, 64 * h:64 * h + 64], False, True)
        mf = None
        for h in range(H):
            mf = MM(psB[:, 64 * h:64 * h + 64], S["arg1"][0:64, :][:, 128 * h:128 * h + 128],
                    S["Sb"][:, 64 * h:64 * h + 64], True, False)
            mf = MM(psB[:, 64 * h:64 * h + 64], S["MT"][:, 128 * h:128 * h + 128],
                    S[Xf][:, 128 * h:128 * h + 64], False, True)
        fmev = op("ac", lambda e: e.copy(out=S["ab"][:], in_=psB[:]),
                  waits={"te": mf["te"]})
        egb = S["egprev"][:].unsqueeze(-1).broadcast_to([128, 16, 64])
        qpart = psA[:].rearrange("p (h z) -> p h z", h=16)[:, :, 0:64]
        ot1 = op("ve", (lambda e: e.tensor_tensor(
            out=S["X1"][:, 0:1024].bitcast(F32)[:, 0:512]
            .rearrange("p (h z) -> p h z", h=16) if False else
            S["KKm"][:].bitcast(F32)[:, 0:1024].rearrange("p (h z) -> p h z", h=16)[:, :, 0:64],
            in0=qpart, in1=egb, op=ALU.mult)),
            waits={"ac": fmev["ac"], "te": mq["te"]})
        oev = op("ve", (lambda e: e.tensor_tensor(
            out=S["cvb"][:, 0:1024],
            in0=S["KKm"][:].bitcast(F32)[:, 0:1024],
            in1=S["ab"][:], op=ALU.add)))


        mb = None
        for h in range(H):
            w = {"ve": oev["ve"]} if h == 0 else None
            mb = MM(psB[0:64, 64 * h:64 * h + 64], S["E1"][0:64, 0:1024][:, 64 * h:64 * h + 64],
                    S["Sb"][:, 64 * h:64 * h + 64], True, False, waits=w)
            mb = MM(psB[0:64, 64 * h:64 * h + 64], S["Kp"][:, 64 * h:64 * h + 64],
                    S[Xf][:, 128 * h:128 * h + 64], False, True)
        gmb = S["gam"][:].unsqueeze(-1).broadcast_to([64, 16, 64])
        op("ve", (lambda e: e.tensor_tensor(
            out=S["Sf"][:].rearrange("p (h z) -> p h z", h=16),
            in0=S["Sf"][:].rearrange("p (h z) -> p h z", h=16),
            in1=gmb, op=ALU.mult)), waits={"te": mb["te"], "ac": gm["ac"]})
        sup = op("ve", (lambda e: e.tensor_tensor(
            out=S["Sf"][:], in0=S["Sf"][:], in1=psB[0:64, :], op=ALU.add)))
        sbc = CP("ve", S["Sb"][:], S["Sf"][:], waits={"ve": sup["ve"]})
        pad = None
        for r in range(4):
            pad = op("ve", (lambda e, r=r: e.tensor_copy(
                out=S["SbE"][32 * r:32 * r + 32, :]
                .rearrange("p (m q) -> p m q", m=4)[:, :, 64 * r:64 * r + 64],
                in_=S["Sb"][0:32, :]
                .rearrange("p (m q) -> p m q", m=4)[:, :, 64 * r:64 * r + 64])))
            pad = op("ve", (lambda e, r=r: e.tensor_copy(
                out=S["SbO"][32 * r:32 * r + 32, :]
                .rearrange("p (m q) -> p m q", m=4)[:, :, 64 * r:64 * r + 64],
                in_=S["Sb"][32:64, :]
                .rearrange("p (m q) -> p m q", m=4)[:, :, 64 * r:64 * r + 64])))

        if phase <= 7:
            sc_end_ve = cnt["ve"]; sc_end_te = cnt["te"]
            continue
        prod = None
        for pr in range(8):
            prod = TT("ve", S["KKm"][:, 128 * pr:128 * pr + 128],
                      S["qsil"][:, 128 * pr:128 * pr + 128],
                      S["ksil"][:, 128 * pr:128 * pr + 128], ALU.mult)
        mmd = None
        for pr in range(8):
            mmd = MM(psS[0:16, 224:352], S["bo4dp"][:, 16 * (pr % 4):16 * (pr % 4) + 16],
                     S["KKm"][:, 128 * pr:128 * pr + 128], pr == 0, pr == 7,
                     waits={"ve": prod["ve"]} if pr == 0 else None)
        qk1 = CP("ve", S["rsq"][:], psS[0:16, 224:352], waits={"te": mmd["te"]})
        qtr = TR(psS[:, 352:368], S["rsq"][:], S["idf32_s"][0:16, 0:16], waits={"ve": qk1["ve"]})
        qk2 = CP("ve", S["qkT"][:], psS[:, 352:368], waits={"te": qtr["te"]})
        qkb = S["qkT"][:].unsqueeze(-1).broadcast_to([128, 16, 64])
        op("ve", (lambda e: e.tensor_tensor(
            out=S["ab"][:].rearrange("p (h z) -> p h z", h=16),
            in0=S["vtm"][:].rearrange("p (h z) -> p h z", h=16),
            in1=qkb, op=ALU.mult)), waits={"ve": qk2["ve"]})
        dpv = TT("ve", S["cvb"][:, 0:1024], S["cvb"][:, 0:1024], S["ab"][:], ALU.add)

        sqo = ACT(S["ab"][:], S["cvb"][:, 0:1024][:], AF.Square, waits={"ve": dpv["ve"]})
        rs1 = op("ve", lambda e: e.tensor_reduce(
            out=S["egprev"][:], in_=S["ab"][:].rearrange("p (h d) -> p h d", h=16),
            axis=AXX, op=ALU.add), waits={"ac": sqo["ac"]})
        rs2 = ACT(S["qkT"][:], S["egprev"][:], AF.Ln, bias=1e-6, scale=1.0 / 64.0,
                  waits={"ve": rs1["ve"]})
        rs3x = ACT(S["egprev"][:], S["qkT"][:], AF.Exp, scale=-0.5)
        rsb = S["egprev"][:].unsqueeze(-1).broadcast_to([128, 16, 64])
        orv = op("ve", (lambda e: e.tensor_tensor(
            out=S["qsil"][:, 0:1024].rearrange("p (h z) -> p h z", h=16),
            in0=S["cvb"][:, 0:1024].rearrange("p (h z) -> p h z", h=16),
            in1=rsb, op=ALU.mult)), waits={"ac": rs3x["ac"]})
        evo = None
        for g in range(2):
            tp = None
            for j in range(4):
                jj = 4 * g + j
                w = {"ve": orv["ve"]} if g == 0 and j == 0 else None
                if evo and j == 0:
                    w = {"ac": evo["ac"]}
                tp = TR(psT[:, 128 * j:128 * j + 128], S["qsil"][:, 128 * jj:128 * jj + 128],
                        S["idbf_s"][:], waits=w)
            evo = op("ac", (lambda e, g=g: e.copy(
                out=S["X0"][:, 512 * g:512 * g + 512], in_=psT[:])),
                waits={"te": tp["te"]})
        slw = ACT(S["X1"][:, 0:1024], S["X0"][:, 0:1024], AF.Silu,
                  scale=S["onw2"][:, 0:1], waits={"ac": evo["ac"]})
        ofd = TT("ve", S["ksil"][:], S["X1"][:, 0:1024], S["gsil"][:], ALU.mult,
                 waits={"ac": slw["ac"]})

        mmo = None
        for j in range(8):
            w = {"ve": ofd["ve"]} if j == 0 else None
            mmo = MM(psB[:, 0:512], S["ksil"][:, 128 * j:128 * j + 128],
                     S["wos"][:, 1024 * j:1024 * j + 512], j == 0, j == 7, waits=w)
            mmo = MM(psB[:, 512:1024], S["ksil"][:, 128 * j:128 * j + 128],
                     S["wos"][:, 1024 * j + 512:1024 * j + 1024], j == 0, j == 7)
        x2d = TT("ve", S["x2t"][:], psB[:], S["xt"][:], ALU.add, waits={"te": mmo["te"]})

        if phase <= 8:
            sc_end_ve = cnt["ve"]; sc_end_te = cnt["te"]
            continue
        ln3 = ln(S["x2t"][:], {"ve": x2d["ve"]})
        yd = ffn(S["x2t"][:], S["xt"][:], ln3)
        od = op("dy", (lambda e, c=c: e.dma_start(
            out=y_out[128 * c:128 * c + 128, :], in_=S["xt"][:])),
            waits={"ve": yd["ve"]}, inc=16)
        last_od = od
        sc_end_ve = yd["ve"]
        sc_end_te = cnt["te"]

    final_dy = cnt["dy"]
    op("dma", (lambda e: e.nop()), waits={"dy": final_dy}, inc=0)

    with nc.Block() as blk:
        engmap = {"te": blk.tensor, "ve": blk.vector, "ac": blk.scalar,
                  "dma": blk.sync, "gp": blk.gpsimd}

        def make(ek):
            def run(e):
                for (w, fn, inc, sk, pre) in sched[ek]:
                    if sk in ("ve", "ac") and pre > 0:
                        e.wait_ge(sems[sk], pre)
                    for k2, v in w.items():
                        e.wait_ge(sems[k2], v)
                    ins = fn(e)
                    if inc:
                        ins.then_inc(sems[sk], inc)
            return run
        for ek in ["te", "ve", "ac", "dma", "gp"]:
            engmap[ek](make(ek))

    for h in reversed(hold):
        h.__exit__(None, None, None)
    return nc


def _host_inputs(inp):
    f32 = np.float32
    A = {k[8:]: np.asarray(v, f32) for k, v in inp.items() if k.startswith("L0_attn_")}
    F = {k[7:]: np.asarray(v, f32) for k, v in inp.items() if k.startswith("L0_ffn_")}
    perm = np.empty(1024, np.int64)
    for r in range(1024):
        eo = r // 512
        rr = r % 512
        a, p = rr // 128, rr % 128
        h, j = 4 * a + p // 32, p % 32
        perm[r] = 64 * h + 2 * j + eo
    d = {}
    d["wq"] = np.ascontiguousarray(A["Wq"][perm].T)
    d["wk"] = np.ascontiguousarray(A["Wk"][perm].T)
    d["wv"] = np.ascontiguousarray(A["Wv"].T)
    d["wg"] = np.ascontiguousarray(A["Wg"].T)
    d["wo"] = np.ascontiguousarray(A["Wo"].T)
    d["fw"] = np.ascontiguousarray(F["weight"].T)
    U2 = F["lora_U"].reshape(128, 1024)
    d["flo"] = np.ascontiguousarray(np.concatenate([U2.T, F["gate_w"].T], axis=1))
    d["fv2"] = np.ascontiguousarray(np.moveaxis(F["lora_V"], 2, 1).reshape(128, 1024))
    d["wba"] = np.ascontiguousarray(np.concatenate(
        [A["Wgk"].T, np.zeros((1024, 16), f32), A["Wb"].T], axis=1))
    cwq = A["cqw"][perm, 0]; cwk = A["ckw"][perm, 0]; cwv = A["cvw"][:, 0]
    cbq = A["cqb"][perm]; cbk = A["ckb"][perm]; cbv = A["cvb"]
    cwh = np.zeros((128, 96), f32)
    cbh = np.zeros((128, 24), f32)
    for bkt in range(8):
        for pi, (wv_, bv_) in enumerate([(cwq, cbq), (cwk, cbk), (cwv, cbv)]):
            cwh[:, 12 * bkt + 4 * pi:12 * bkt + 4 * pi + 4] = wv_[128 * bkt:128 * bkt + 128]
            cbh[:, 3 * bkt + pi] = bv_[128 * bkt:128 * bkt + 128]
    d["cw"] = cwh; d["cb"] = cbh
    d["smalls"] = np.ascontiguousarray(np.stack(
        [A["bgk"], A["dt_bias"], A["bb"], A["A_log"], A["Dp"]], axis=1))
    d["onw"] = np.ascontiguousarray(A["onorm_w"][:, None])
    st = np.triu(np.ones((128, 128), f32), 1)
    d["maskneg"] = -st
    d["maskpos"] = st
    dn = np.zeros((16, 512), NPBF)
    for a in range(4):
        for p in range(128):
            dn[4 * a + p // 32, 128 * a + p] = 1
    d["dn4"] = dn
    bo4t = np.ascontiguousarray(dn.astype(f32).reshape(16, 4, 128)
                                .transpose(2, 1, 0).reshape(128, 64))
    d["bo4f"] = bo4t
    d["bo4b"] = bo4t.astype(NPBF)
    d["ones3"] = np.ones((3, 128), NPBF)
    d["ones1f"] = np.ones((1, 128), f32)
    d["eye16f"] = np.eye(16, dtype=f32)
    d["idbf"] = np.eye(128, dtype=NPBF)
    d["idf32"] = np.eye(128, dtype=f32)
    invf = (1.0 / 10000.0 ** (np.arange(0, D, 2, dtype=f32) / D))
    tt = np.arange(T, dtype=f32)
    j32 = (2 * (np.arange(128) % 32)) % 32
    ang = np.outer(invf[j32], tt)
    d["cs"] = np.ascontiguousarray(
        np.concatenate([np.cos(ang), np.sin(ang)], axis=1).astype(NPBF))
    return d


def kernel(**inputs):
    shared = _host_inputs(inputs)
    x = np.asarray(inputs["L0_attn_x"], np.float32)
    if "nc" not in _cache:
        _cache["nc"] = build_nc()
    nc = _cache["nc"]
    in_maps = []
    for b in range(8):
        m = dict(shared)
        m["x"] = np.ascontiguousarray(x[b])
        in_maps.append(m)
    res = run_bass_kernel_spmd(nc, in_maps, core_ids=list(range(8)))
    out = np.stack([r["y"] for r in res.results], axis=0)
    return out.astype(np.float32)
